# version 05
# baseline (speedup 1.0000x reference)
import numpy as np
import ml_dtypes
from contextlib import ExitStack

import concourse.bass as bass
import concourse.mybir as mybir
from concourse.bass_utils import run_bass_kernel_spmd
from concourse.alu_op_type import AluOpType

F32 = mybir.dt.float32
BF16 = mybir.dt.bfloat16
I32 = mybir.dt.int32
AF = mybir.ActivationFunctionType
AX = mybir.AxisListType
BF16NP = ml_dtypes.bfloat16

DM, DMC = 4096, 32
S, OWN, NT, NTS = 2048, 1024, 8, 16
NH = 32
DFF, NS = 11008, 86
R = 16
EPS = 1e-5
MASKV = -66504.0
SCALE = 1.0 / float(np.sqrt(np.float64(128.0)))

NF4 = np.array([-1.0, -0.6961928009986877, -0.5250730514526367,
                -0.39491748809814453, -0.28444138169288635,
                -0.18477343022823334, -0.09105003625154495, 0.0,
                0.07958029955625534, 0.16093020141124725, 0.2461123019456863,
                0.33791524171829224, 0.44070982933044434, 0.5626170039176941,
                0.7229568362236023, 1.0], dtype=np.float32)
DELT = np.diff(NF4.astype(np.float64)).astype(np.float32)
GP_FIRST = 7
GU_F = 2048
D_CHUNKS = [1408] * 7 + [1152]
D_COFF = [sum(D_CHUNKS[:i]) for i in range(8)]

ENGS = ("sync", "tensor", "vector", "scalar", "gpsimd")
import os as _os
DEBUG_TAPS = bool(_os.environ.get("KDBG"))


def build_program():
    nc = bass.Bass("TRN2", target_bir_lowering=False, debug=False)

    def dram(name, shape, dt, kind=None):
        if kind:
            return nc.dram_tensor(name, shape, dt, kind=kind).ap()
        return nc.dram_tensor(name, shape, dt).ap()

    hs = dram("hs", [S, DM], F32, "ExternalInput")
    qkv_w = dram("qkv_w", [DM, 3 * DM], F32, "ExternalInput")
    qkv_b = dram("qkv_b", [3 * DM], F32, "ExternalInput")
    out_w = dram("out_w", [DM, DM], F32, "ExternalInput")
    out_b = dram("out_b", [DM], F32, "ExternalInput")
    g_codes = dram("g_codes", [DFF, DM], I32, "ExternalInput")
    g_absmax = dram("g_absmax", [DFF, DM // 64], F32, "ExternalInput")
    u_codes = dram("u_codes", [DFF, DM], I32, "ExternalInput")
    u_absmax = dram("u_absmax", [DFF, DM // 64], F32, "ExternalInput")
    d_codes = dram("d_codes", [DM, DFF], I32, "ExternalInput")
    d_absmax = dram("d_absmax", [DM, DFF // 64], F32, "ExternalInput")
    g_la = dram("g_la", [DM, R], F32, "ExternalInput")
    g_lb = dram("g_lb", [R, DFF], F32, "ExternalInput")
    u_la = dram("u_la", [DM, R], F32, "ExternalInput")
    u_lb = dram("u_lb", [R, DFF], F32, "ExternalInput")
    d_la = dram("d_la", [DFF, R], F32, "ExternalInput")
    d_lb = dram("d_lb", [R, DM], F32, "ExternalInput")
    kpos = dram("kpos", [128, S], F32, "ExternalInput")
    jcol = dram("jcol", [128, 1], F32, "ExternalInput")
    cmask = dram("cmask", [1, 128], BF16, "ExternalInput")
    ones_row = dram("ones_row", [1, 512], BF16, "ExternalInput")
    out = dram("out", [OWN, DM], F32, "ExternalOutput")

    xT_sc = dram("xT_sc", [NTS, DMC, 128, 128], BF16)
    qkvT_sc = dram("qkvT_sc", [96, NTS, 128, 128], BF16)
    wgT_sc = dram("wgT_sc", [NS, DMC, 128, 128], BF16)
    wuT_sc = dram("wuT_sc", [NS, DMC, 128, 128], BF16)
    wdT_sc = dram("wdT_sc", [NS, DMC, 128, 128], BF16)
    x3_sc = dram("x3_sc", [NT, NS, 128, 128], BF16)
    qkvw_bf = dram("qkvw_bf", [96, DMC, 128, 128], BF16)
    attn_sc = dram("attn_sc", [OWN, DM], F32)
    x1_sc = dram("x1_sc", [OWN, DM], F32)
    res2_sc = dram("res2_sc", [OWN, DM], F32)

    es = ExitStack()

    def sbuf(name, shape, dt):
        return es.enter_context(nc.sbuf_tensor(name, shape, dt))

    identb = sbuf("identb", [128, 128], BF16)
    identf = sbuf("identf", [128, 128], F32)
    triag = sbuf("triag", [128, 128], F32)
    invf = sbuf("invf", [128, 1], F32)
    biasq = sbuf("biasq", [128, 96], F32)
    biaso = sbuf("biaso", [128, 32], F32)
    onesb = sbuf("onesb", [1, 512], BF16)
    cmb = sbuf("cmb", [1, 128], BF16)
    amt = sbuf("amt", [128, 64], F32)
    lora_a = sbuf("lora_a", [128, 16 * 150], BF16)
    lbs = [sbuf(f"lbs{i}", [16, 128], BF16) for i in range(2)]
    lbf = [sbuf(f"lbf{i}", [16, 128], F32) for i in range(2)]
    tmp_l = sbuf("tmp_l", [16, 2 * OWN], BF16)
    tmp_d = sbuf("tmp_d", [16, OWN], F32)
    tmp_db = sbuf("tmp_db", [16, OWN], BF16)
    dlb_t = sbuf("dlb_t", [16, 512], BF16)
    dlb_f = sbuf("dlb_f", [16, 512], F32)
    stat = sbuf("stat", [128, 16], F32)
    st_sum = sbuf("st_sum", [128, NT], F32)
    st_sq = sbuf("st_sq", [128, NT], F32)

    A_bf = sbuf("A_bf", [128, 49152], BF16)
    A_f1 = sbuf("A_f1", [128, 12288], F32)
    A_f2 = sbuf("A_f2", [128, 8192], F32)
    A_bf_i = A_bf.ap().bitcast(I32)
    A_f2_b = A_f2.ap().bitcast(BF16)
    lora_as = A_f1[:, 4096:4096 + 16 * NS]
    swp = A_f2_b[:, 0:S + OWN]

    ps_all = es.enter_context(nc.psum_tensor("ps_all", [128, 4096], F32))

    def bank(b, w=1):
        return ps_all[:, b * 512:(b + w) * 512]

    streams = {e: [] for e in ENGS}
    sems = {}

    class _S:
        __slots__ = ("h", "n")

        def __init__(self, h):
            self.h = h
            self.n = 0

    def sem(name):
        if name not in sems:
            sems[name] = _S(es.enter_context(nc.semaphore(name)))
        return sems[name]

    def op(eng, emit, inc=None, k=1, wait=None):
        waits = []
        for w in (wait or []):
            nm, val = w
            if val > 0:
                waits.append((sem(nm).h, val))
        inch = None
        val_out = None
        if inc is not None:
            s = sem(inc)
            s.n += k
            val_out = s.n
            inch = s.h

        def thunk(E, emit=emit, waits=waits, inch=inch, k=k):
            for hh, vv in waits:
                E.wait_ge(hh, vv)
            inst = emit(E)
            if inch is not None:
                inst.then_inc(inch, k)
        streams[eng].append(thunk)
        return val_out

    def dma(outap, inap, inc="ld", wait=None, eng="sync"):
        return op(eng, lambda E: E.dma_start(out=outap, in_=inap), inc=inc,
                  k=16, wait=wait)

    def cur(name):
        return sem(name).n

    trc = [0]

    def transpose_block(src_ap, dst_ap, ident, wait_t=None, wait_s=None):
        i = trc[0]
        b = 4 + (i % 4)
        if src_ap.dtype == BF16:
            pv = bank(b).bitcast(BF16)[:, 0:128]
        else:
            pv = bank(b)[:, 0:128]
        wt = list(wait_t or [])
        if i >= 4:
            wt.append(("we", i - 3))
        op("tensor", lambda E, s=src_ap, idn=ident, pv=pv: E.transpose(pv, s, idn),
           inc="wt", wait=wt)
        ws = list(wait_s or [])
        ws.append(("wt", i + 1))
        ev = op("scalar", lambda E, d=dst_ap, pv=pv: E.copy(d, pv),
                inc="we", wait=ws)
        trc[0] += 1
        return ev

    def g_consts(E):
        E.memset(identb[:], 0.0)
        E.affine_select(out=identb[:], in_=identb[:], compare_op=AluOpType.not_equal,
                        fill=1.0, base=0, pattern=[[-1, 128]], channel_multiplier=1)
        E.memset(identf[:], 0.0)
        E.affine_select(out=identf[:], in_=identf[:], compare_op=AluOpType.not_equal,
                        fill=1.0, base=0, pattern=[[-1, 128]], channel_multiplier=1)
        E.memset(triag[:], 0.0)
        E.affine_select(out=triag[:], in_=triag[:], compare_op=AluOpType.is_ge,
                        fill=MASKV, base=0, pattern=[[-1, 128]], channel_multiplier=1)
        E.memset(tmp_d[:], 0.0)
        return E.engine_nop()
    op("gpsimd", g_consts, inc="gg")

    jld = dma(invf[:, 0:1], jcol[:, :])
    op("scalar", lambda E: E.activation(out=invf[:, 0:1], in_=invf[:, 0:1],
                                        func=AF.Exp,
                                        scale=-float(np.log(10000.0) / 64.0)),
       inc="ev", wait=[("ld", jld)])

    dma(biasq[:, :], qkv_b.rearrange("(c p) -> p c", p=128))
    dma(biaso[:, :], out_b.rearrange("(c p) -> p c", p=128))
    dma(onesb[:, :], ones_row[:, :])
    dma(cmb[:, :], cmask[:, :])
    dma(lora_as[:, 0:16 * DMC].rearrange("p (c r) -> p c r", r=16),
        g_la.rearrange("(c p) r -> p c r", p=128))
    dma(lora_as[:, 16 * DMC:32 * DMC].rearrange("p (c r) -> p c r", r=16),
        u_la.rearrange("(c p) r -> p c r", p=128))
    op("vector", lambda E: E.tensor_copy(lora_a[:, 0:16 * 64], lora_as[:, 0:16 * 64]),
       inc="dv", wait=[("ld", cur("ld"))])
    dma(lora_as[:, 0:16 * NS].rearrange("p (c r) -> p c r", r=16),
        d_la.rearrange("(c p) r -> p c r", p=128), wait=[("dv", cur("dv"))])
    op("vector", lambda E: E.tensor_copy(lora_a[:, 16 * 64:16 * 150],
                                         lora_as[:, 0:16 * NS]),
       inc="dv", wait=[("ld", cur("ld"))])

    ktab_c = A_f1[:, 0:S]
    ktab_s = A_f1[:, S:2 * S]
    angt = A_f1[:, 2 * S:3 * S]
    rop1 = A_f1[:, 3 * S:4 * S]
    rop2 = A_f1[:, 4 * S:5 * S]
    kposb = A_f1[:, 5 * S:6 * S]

    kldv = dma(kposb, kpos[:, :])

    def v_ang(E):
        E.tensor_mul(angt, kposb, invf.ap()[:, 0:1].broadcast_to((128, S)))
        inv2pi = float(1.0 / (2 * np.pi))
        E.tensor_scalar(out=rop2, in0=angt, scalar1=inv2pi, scalar2=0.5,
                        op0=AluOpType.mult, op1=AluOpType.add)
        E.tensor_copy(rop1.bitcast(I32), rop2)
        E.tensor_copy(rop1, rop1.bitcast(I32))
        C1 = float(np.float32(2 * np.pi))
        C2 = float(np.float64(2 * np.pi) - np.float64(np.float32(2 * np.pi)))
        E.scalar_tensor_tensor(out=angt, in0=rop1, scalar=-C1, in1=angt,
                               op0=AluOpType.mult, op1=AluOpType.add)
        E.scalar_tensor_tensor(out=angt, in0=rop1, scalar=-C2, in1=angt,
                               op0=AluOpType.mult, op1=AluOpType.add)
        E.tensor_scalar(out=rop2, in0=angt, scalar1=float(np.pi),
                        scalar2=float(-2 * np.pi), op0=AluOpType.is_gt,
                        op1=AluOpType.mult)
        E.tensor_add(angt, angt, rop2)
        E.tensor_scalar(out=rop2, in0=angt, scalar1=float(-np.pi),
                        scalar2=float(2 * np.pi), op0=AluOpType.is_lt,
                        op1=AluOpType.mult)
        return E.tensor_add(angt, angt, rop2)
    op("vector", v_ang, inc="dv", wait=[("ld", kldv), ("ev", cur("ev"))])
    op("scalar", lambda E: E.activation(out=ktab_s, in_=angt, func=AF.Sin),
       inc="ev", wait=[("dv", cur("dv"))])

    def v_cosarg(E):
        E.tensor_scalar(out=rop1, in0=angt, scalar1=float(np.pi / 2), scalar2=None,
                        op0=AluOpType.add)
        E.tensor_scalar(out=rop2, in0=rop1, scalar1=float(np.pi),
                        scalar2=float(-2 * np.pi), op0=AluOpType.is_gt,
                        op1=AluOpType.mult)
        return E.tensor_add(rop1, rop1, rop2)
    op("vector", v_cosarg, inc="dv", wait=[("ev", cur("ev"))])
    op("scalar", lambda E: E.activation(out=ktab_c, in_=rop1, func=AF.Sin),
       inc="ev", wait=[("dv", cur("dv"))])

    xin = [A_f2[:, 0:4096], A_f2[:, 4096:8192]]
    xstg = [A_bf[:, 0:4096], A_bf[:, 4096:8192]]
    p1_st = {}
    p1_tr_end = {}
    for tt in range(NTS):
        w = []
        if tt >= 2:
            w.append(("we", p1_tr_end[tt - 2]))
            w.append(("st", p1_st[tt - 2]))
        ldv = dma(xin[tt % 2], hs[tt * 128:(tt + 1) * 128, :], wait=w)
        ev = None
        for k in range(DMC):
            ev = transpose_block(
                xin[tt % 2][:, k * 128:(k + 1) * 128],
                xstg[tt % 2].rearrange("p (k c) -> p k c", c=128)[:, k, :],
                identf[:],
                wait_t=[("ld", ldv)] if k == 0 else None)
        p1_tr_end[tt] = ev
        p1_st[tt] = dma(xT_sc[tt].rearrange("k p c -> p k c"),
                        xstg[tt % 2].rearrange("p (k c) -> p k c", c=128),
                        inc="st", wait=[("we", ev)])
    ST_P1 = cur("st")

    xTh = A_bf[:, 8192:8192 + DMC * 1024]
    wcol = [A_f2[:, 0:4096], A_f2[:, 4096:8192]]
    wbf = [A_bf[:, 0:4096], A_bf[:, 4096:8192]]
    qstg = [A_bf[:, 40960:40960 + 1024], A_bf[:, 41984:41984 + 1024]]
    ofs_half = [list(range(96)), list(range(32, 96))]
    p2_cast, p2_mm, p2_ev, p2_st = {}, {}, {}, {}
    for half in range(2):
        ofl = ofs_half[half]
        xw = [("st", ST_P1)]
        if half == 1:
            xw.append(("pe", p2_mm[(0, 95)]))
        xldv = None
        for t8 in range(8):
            xldv = dma(
                xTh.rearrange("p (k t c) -> p t k c", k=DMC, c=128)[:, t8],
                xT_sc[half * 8 + t8].rearrange("k p c -> p k c"),
                wait=xw if t8 == 0 else None)
        for j, of in enumerate(ofl):
            key = (half, of)
            wld = []
            if j >= 2:
                wld.append(("pe", p2_mm[(half, ofl[j - 2])]))
            if half == 0:
                wl = dma(wcol[j % 2].rearrange("p (k c) -> p k c", c=128),
                         qkv_w.rearrange("(k p) (o c) -> o k p c", p=128, c=128)[of].rearrange("k p c -> p k c"),
                         wait=wld)
                p2_cast[key] = op(
                    "vector", lambda E, j=j: E.tensor_copy(wbf[j % 2], wcol[j % 2]),
                    inc="dv", wait=[("ld", wl)] + wld)
                dma(qkvw_bf[of].rearrange("k p c -> p k c"),
                    wbf[j % 2].rearrange("p (k c) -> p k c", c=128),
                    inc="st", wait=[("dv", p2_cast[key])])
                wsrc = ("dv", p2_cast[key])
            else:
                wl = dma(wbf[j % 2].rearrange("p (k c) -> p k c", c=128),
                         qkvw_bf[of].rearrange("k p c -> p k c"), wait=wld)
                wsrc = ("ld", wl)

            def t_mm(E, j=j):
                pb = (j % 2) * 2
                last = None
                for k in range(DMC):
                    lhs = wbf[j % 2][:, k * 128:(k + 1) * 128]
                    for tg in range(2):
                        last = E.matmul(bank(pb + tg), lhs,
                                        xTh[:, k * 1024 + tg * 512:k * 1024 + (tg + 1) * 512],
                                        start=(k == 0), stop=(k == DMC - 1))
                return last
            wmm = [("ld", xldv), wsrc]
            if j >= 2:
                wmm.append(("ev", p2_ev[(half, ofl[j - 2])]))
            p2_mm[key] = op("tensor", t_mm, inc="pe", wait=wmm)

            def s_ev_f(E, j=j, of=of):
                pb = (j % 2) * 2
                last = None
                for tg in range(2):
                    last = E.tensor_add(
                        qstg[j % 2].rearrange("p (t c) -> p t c", c=512)[:, tg, :],
                        bank(pb + tg),
                        biasq.ap()[:, of:of + 1].broadcast_to((128, 512)))
                return last
            wev = [("pe", p2_mm[key])]
            if j >= 2:
                wev.append(("st", p2_st[(half, ofl[j - 2])]))
            p2_ev[key] = op("vector", s_ev_f, inc="ev", wait=wev)
            p2_st[key] = dma(qkvT_sc[of, half * 8:half * 8 + 8].rearrange("t p c -> p t c"),
                             qstg[j % 2].rearrange("p (t c) -> p t c", c=128),
                             inc="st", wait=[("ev", p2_ev[key])])
    ST_P2 = cur("st")

    kb = [A_bf[:, 0:2048], A_bf[:, 2048:4096]]
    qb = [A_bf[:, 4096:5120], A_bf[:, 5120:6144]]
    vbt = [A_bf[:, 6144:8192], A_bf[:, 8192:10240]]
    pbuf = A_bf[:, 10240:12288]
    ptb = A_bf[:, 12288:14336]
    ctx_big = A_bf[:, 14336:14336 + NH * 1024]
    rpA = A_f1[:, 2 * S:3 * S]
    rpB = A_f1[:, 3 * S:4 * S]

    p3_rope, p3_cev = {}, {}
    dbg_hold = []
    pv_last = 0
    if DEBUG_TAPS:
        d_krope = dram("dbg_krope", [128, S], BF16, "ExternalOutput")
        d_qrope = dram("dbg_qrope", [128, OWN], BF16, "ExternalOutput")
        d_p00 = dram("dbg_p00", [128, S], BF16, "ExternalOutput")
        d_vt0 = dram("dbg_vt0", [128, S], BF16, "ExternalOutput")
        d_ctx0 = dram("dbg_ctx0", [128, OWN], BF16, "ExternalOutput")
        d_s00 = dram("dbg_s00", [128, S], F32, "ExternalOutput")
    for h in range(NH):
        e = h % 2
        w = [("st", ST_P2)]
        if h >= 2:
            w.append(("pe", pv_last))
            w.append(("ev", p3_cev[h - 2]))
        dma(kb[e].rearrange("p (t c) -> p t c", c=128),
            qkvT_sc[32 + h].rearrange("t p c -> p t c"), wait=w)
        dma(qb[e].rearrange("p (t c) -> p t c", c=128),
            qkvT_sc[h, 0:8].rearrange("t p c -> p t c"))
        ldv = dma(vbt[e].rearrange("p (t c) -> p t c", c=128),
                  qkvT_sc[64 + h].rearrange("t p c -> p t c"))
        sw = [("ld", ldv)]
        if h >= 1:
            sw.append(("dv", p3_rope[h - 1]))
        dma(swp[0:64, 0:S], kb[e][64:128, 0:S], wait=sw)
        dma(swp[64:128, 0:S], kb[e][0:64, 0:S])
        dma(swp[0:64, S:S + OWN], qb[e][64:128, 0:OWN])
        swv = dma(swp[64:128, S:S + OWN], qb[e][0:64, 0:OWN])

        def v_rope(E, e=e):
            E.tensor_mul(rpA[0:64, 0:S], kb[e][0:64, 0:S], ktab_c[0:64, 0:S])
            E.tensor_mul(rpB[0:64, 0:S], swp[0:64, 0:S], ktab_s[0:64, 0:S])
            E.tensor_sub(kb[e][0:64, 0:S], rpA[0:64, 0:S], rpB[0:64, 0:S])
            E.tensor_mul(rpA[64:128, 0:S], kb[e][64:128, 0:S], ktab_c[64:128, 0:S])
            E.tensor_mul(rpB[64:128, 0:S], swp[64:128, 0:S], ktab_s[64:128, 0:S])
            E.tensor_add(kb[e][64:128, 0:S], rpA[64:128, 0:S], rpB[64:128, 0:S])
            E.tensor_mul(rpA[0:64, 0:OWN], qb[e][0:64, 0:OWN], ktab_c[0:64, 0:OWN])
            E.tensor_mul(rpB[0:64, 0:OWN], swp[0:64, S:S + OWN], ktab_s[0:64, 0:OWN])
            E.tensor_sub(qb[e][0:64, 0:OWN], rpA[0:64, 0:OWN], rpB[0:64, 0:OWN])
            E.tensor_mul(rpA[64:128, 0:OWN], qb[e][64:128, 0:OWN], ktab_c[64:128, 0:OWN])
            E.tensor_mul(rpB[64:128, 0:OWN], swp[64:128, S:S + OWN], ktab_s[64:128, 0:OWN])
            return E.tensor_add(qb[e][64:128, 0:OWN], rpA[64:128, 0:OWN],
                                rpB[64:128, 0:OWN])
        p3_rope[h] = op("vector", v_rope, inc="dv", wait=[("ld", swv)])
        if DEBUG_TAPS and h == 0:
            dma(d_krope, kb[e][:, 0:S], inc="st", wait=[("dv", p3_rope[h])])
            dma(d_qrope, qb[e][:, 0:OWN], inc="st", wait=[("dv", p3_rope[h])])

        vev = None
        for j in range(NTS):
            vev = transpose_block(
                vbt[e][:, j * 128:(j + 1) * 128],
                vbt[e].rearrange("p (t c) -> p t c", c=128)[:, j, :],
                identb[:],
                wait_t=[("dv", p3_rope[h])] if j == 0 else None)

        if DEBUG_TAPS and h == 0:
            dma(d_vt0, vbt[e][:, 0:S], inc="st", wait=[("we", vev)])
        for qt in range(NT):
            eo = (qt + 1) * 128

            def t_scores(E, e=e, qt=qt, eo=eo):
                lhs = qb[e][:, qt * 128:(qt + 1) * 128]
                for gi in range((eo + 511) // 512):
                    c0 = gi * 512
                    c1 = min(eo, c0 + 512)
                    E.matmul(ps_all[:, c0:c1], lhs, kb[e][:, c0:c1],
                             start=True, stop=True)
                last = None
                for gi in range(2):
                    c0 = 1024 + gi * 512
                    E.matmul(ps_all[:, c0:c0 + 512], lhs, kb[e][:, c0:c0 + 512],
                             start=True, stop=False)
                    last = E.matmul(ps_all[:, c0:c0 + 512], cmb[:, :], onesb[:, :],
                                    start=False, stop=True)
                return last
            wsc = []
            if qt == 0:
                wsc += [("dv", p3_rope[h]), ("we", vev)]
            else:
                wsc.append(("ev", p3_cev[(h, qt - 1)]))
            wsc += dbg_hold
            dbg_hold.clear() if dbg_hold else None
            mmv = op("tensor", t_scores, inc="pe", wait=wsc)

            mskv = op("vector",
                      lambda E, qt=qt: E.tensor_add(
                          ps_all[:, qt * 128:(qt + 1) * 128],
                          ps_all[:, qt * 128:(qt + 1) * 128], triag[:]),
                      inc="dv", wait=[("pe", mmv)])

            def s_exp(E, eo=eo):
                E.activation(out=rpA[:, 0:eo], in_=ps_all[:, 0:eo], func=AF.Exp,
                             scale=SCALE, accum_out=stat[:, 0:1])
                return E.activation(out=rpA[:, 1024:2048], in_=ps_all[:, 1024:2048],
                                    func=AF.Exp, scale=SCALE, accum_out=stat[:, 1:2])
            expv = op("scalar", s_exp, inc="ev", wait=[("dv", mskv)])

            sumv = op("vector",
                      lambda E: E.tensor_add(stat[:, 2:3], stat[:, 0:1],
                                             stat[:, 1:2]),
                      inc="dv", wait=[("ev", expv)])

            def s_recip(E):
                E.activation(out=stat[:, 4:5], in_=stat[:, 2:3], func=AF.Ln)
                return E.activation(out=stat[:, 3:4], in_=stat[:, 4:5],
                                    func=AF.Exp, scale=-1.0)
            rcpv = op("scalar", s_recip, inc="ev", wait=[("dv", sumv)])

            def v_norm(E, eo=eo):
                E.tensor_mul(pbuf[:, 0:eo], rpA[:, 0:eo],
                             stat.ap()[:, 3:4].broadcast_to((128, eo)))
                return E.tensor_mul(pbuf[:, 1024:2048], rpA[:, 1024:2048],
                                    stat.ap()[:, 3:4].broadcast_to((128, 1024)))
            nrmv = op("vector", v_norm, inc="dv", wait=[("ev", rcpv)])
            if DEBUG_TAPS and h == 0 and qt == 0:
                svev = op("scalar", lambda E: E.copy(rop2, ps_all[:, 0:S]),
                          inc="ev", wait=[("dv", mskv)])
                dbg_hold.append(("st", dma(d_s00, rop2, inc="st",
                                           wait=[("ev", svev)])))
                dbg_hold.append(("st", dma(d_p00, pbuf[:, 0:S], inc="st",
                                           wait=[("dv", nrmv)])))

            cols = [j * 128 for j in range(eo // 128)] + \
                   [1024 + j * 128 for j in range(8)]
            lastev = None
            for j, col in enumerate(cols):
                lastev = transpose_block(
                    pbuf[:, col:col + 128],
                    ptb.rearrange("p (t c) -> p t c", c=128)[:, j, :],
                    identb[:],
                    wait_t=[("dv", nrmv)] if j == 0 else None)

            def t_pv(E, e=e, cols=cols):
                last = None
                for j, col in enumerate(cols):
                    last = E.matmul(bank(0)[:, 0:128], vbt[e][:, col:col + 128],
                                    ptb[0:128, j * 128:(j + 1) * 128],
                                    start=(j == 0), stop=(j == len(cols) - 1))
                return last
            pv_last = op("tensor", t_pv, inc="pe", wait=[("we", lastev)])
            p3_cev[(h, qt)] = op(
                "scalar",
                lambda E, h=h, qt=qt: E.copy(
                    ctx_big[:, h * 1024 + qt * 128:h * 1024 + (qt + 1) * 128],
                    bank(0)[:, 0:128]),
                inc="ev", wait=[("pe", pv_last)])
        if DEBUG_TAPS and h == 0:
            dma(d_ctx0, ctx_big[:, 0:OWN], inc="st",
                wait=[("ev", p3_cev[(0, NT - 1)])])
        p3_cev[h] = p3_cev[(h, NT - 1)]
    P3_EV = cur("ev")

    wcol4 = [A_f2[:, 0:4096], A_f2[:, 4096:8192]]
    wbf4 = [A_bf[:, 0:4096], A_bf[:, 4096:8192]]
    attn_ev = A_f1[:, 0:1024]
    tstg4 = A_f1[:, 1024:1152]
    p4_mm, p4_ev = {}, {}
    p4_blk_st = {}
    for of in range(DMC):
        wld = []
        if of >= 2:
            wld.append(("pe", p4_mm[of - 2]))
        else:
            wld.append(("pe", pv_last))
            wld.append(("ev", P3_EV))
        wl = dma(wcol4[of % 2].rearrange("p (k c) -> p k c", c=128),
                 out_w.rearrange("(k p) (o c) -> o k p c", p=128, c=128)[of].rearrange("k p c -> p k c"),
                 wait=wld)
        cv = op("vector", lambda E, of=of: E.tensor_copy(wbf4[of % 2], wcol4[of % 2]),
                inc="dv", wait=[("ld", wl)] + wld)

        def t_op(E, of=of):
            pb = (of % 2) * 2
            last = None
            for k in range(DMC):
                lhs = wbf4[of % 2][:, k * 128:(k + 1) * 128]
                for tg in range(2):
                    last = E.matmul(bank(pb + tg), lhs,
                                    ctx_big[:, k * 1024 + tg * 512:k * 1024 + (tg + 1) * 512],
                                    start=(k == 0), stop=(k == DMC - 1))
            return last
        wmm = [("dv", cv), ("ev", P3_EV)]
        if of >= 2:
            wmm.append(("ev", p4_ev[of - 2]))
        p4_mm[of] = op("tensor", t_op, inc="pe", wait=wmm)

        def s_ev4(E, of=of):
            pb = (of % 2) * 2
            last = None
            for tg in range(2):
                last = E.tensor_add(
                    attn_ev.rearrange("p (t c) -> p t c", c=512)[:, tg, :],
                    bank(pb + tg),
                    biaso.ap()[:, of:of + 1].broadcast_to((128, 512)))
            return last
        wev = [("pe", p4_mm[of])]
        if of >= 1:
            wev.append(("st", p4_blk_st[of - 1]))
        p4_ev[of] = op("vector", s_ev4, inc="ev", wait=wev)

        for j in range(NT):
            ev = transpose_block(
                attn_ev.rearrange("p (t c) -> p t c", c=128)[:, j, :],
                tstg4, identf[:],
                wait_t=[("ev", p4_ev[of])] if j == 0 else
                       [("st", cur("st"))])
            p4_blk_st[of] = dma(
                attn_sc.rearrange("(t p) (o c) -> o t p c", p=128, c=128)[of, j],
                tstg4, inc="st", wait=[("we", ev)])

    x1T = A_bf[:, 14336:14336 + DMC * 1024]
    x1bf = A_bf[:, 0:4096]
    ln_a = A_f2[:, 0:4096]
    ln_b = A_f2[:, 4096:8192]
    res_t = A_f1[:, 2048:6144]
    cent = A_f1[:, 6144:10240]
    P4_ST_ALL = cur("st")
    for tt in range(NT):
        la = dma(ln_a, attn_sc[tt * 128:(tt + 1) * 128, :],
                 wait=[("st", P4_ST_ALL)] if tt == 0 else [("dv", cur("dv"))])
        lb = dma(ln_b, hs[tt * 128:(tt + 1) * 128, :])

        dv0 = op("vector", lambda E: E.tensor_add(res_t, ln_a, ln_b),
                 inc="dv", wait=[("ld", lb)])
        sv0 = op("scalar", lambda E: E.activation(out=ln_a, in_=res_t, func=AF.Copy,
                                                  accum_out=stat[:, 0:1]),
                 inc="ev", wait=[("dv", dv0)])

        dvs = op("vector", lambda E: E.tensor_scalar(
            out=stat[:, 1:2], in0=stat[:, 0:1], scalar1=float(1.0 / DM),
            scalar2=None, op0=AluOpType.mult), inc="dv", wait=[("ev", sv0)])
        gdv = op("gpsimd", lambda E: E.tensor_copy(stat[:, 6:7], stat[:, 1:2]),
                 inc="gg", wait=[("dv", dvs)])
        dvv = op("vector", lambda E: E.tensor_sub(
            cent, res_t, stat.ap()[:, 6:7].broadcast_to((128, DM))),
            inc="dv", wait=[("gg", gdv)])

        sq = op("scalar", lambda E: E.activation(out=ln_a, in_=cent, func=AF.Square,
                                                 accum_out=stat[:, 2:3]),
                inc="ev", wait=[("dv", dvv)])

        def v_var(E):
            E.tensor_scalar(out=stat[:, 3:4], in0=stat[:, 2:3],
                            scalar1=float(1.0 / DM), scalar2=float(EPS),
                            op0=AluOpType.mult, op1=AluOpType.add)
            return E.engine_nop()
        vv = op("vector", v_var, inc="dv", wait=[("ev", sq)])
        def s_rsq(E):
            E.activation(out=stat[:, 4:5], in_=stat[:, 3:4], func=AF.Ln)
            return E.activation(out=stat[:, 5:6], in_=stat[:, 4:5], func=AF.Exp,
                                scale=-0.5)
        sqv = op("scalar", s_rsq, inc="ev", wait=[("dv", vv)])

        def v_x1(E):
            E.tensor_mul(ln_b, cent, stat.ap()[:, 5:6].broadcast_to((128, DM)))
            return E.tensor_copy(x1bf, ln_b)
        x1v = op("vector", v_x1, inc="dv",
                 wait=[("ev", sqv), ("we", cur("we"))])
        dma(x1_sc[tt * 128:(tt + 1) * 128, :], ln_b, inc="st", wait=[("dv", x1v)])
        for k in range(DMC):
            transpose_block(
                x1bf[:, k * 128:(k + 1) * 128],
                x1T.rearrange("p (k t c) -> p k t c", k=DMC, c=128)[:, k, tt, :],
                identb[:],
                wait_t=[("dv", x1v)] if k == 0 else None)
    LN1_WE = cur("we")

    deq_tiles = []
    for s in range(NS):
        for mat in ("g", "u"):
            for jh in range(2):
                deq_tiles.append((mat, s, jh))
    for rs in range(DMC):
        for cc in range(8):
            deq_tiles.append(("d", rs, cc))

    dq_codes = [A_bf_i[:, 0:2048], A_bf_i[:, 2048:4096]]
    dq_w = [A_bf[:, 8192:10240], A_bf[:, 10240:12288]]
    acc = [A_f1[:, 0:2048], A_f1[:, 2048:4096]]
    term = A_f1[:, 4096:6144]
    gterm = [A_f1[:, 6144:8192], A_f1[:, 8192:10240]]
    gacc = [A_f2[:, 0:2048], A_f2[:, 2048:4096]]

    deq_done = {}
    gu_ws = {}
    prev_wv = {}

    n_gp = 15 - GP_FIRST
    part2 = {}
    for ti, tile in enumerate(deq_tiles):
        kind = tile[0]
        if kind in ("g", "u"):
            _, s, jh = tile
            codes_src = (g_codes if kind == "g" else u_codes)[
                s * 128:(s + 1) * 128, jh * GU_F:(jh + 1) * GU_F]
            am_src = (g_absmax if kind == "g" else u_absmax)[
                s * 128:(s + 1) * 128, jh * (GU_F // 64):(jh + 1) * (GU_F // 64)]
            F = GU_F
            wt_dst = (wgT_sc if kind == "g" else wuT_sc)[s, jh * 16:jh * 16 + 16]
        else:
            _, rs, cc = tile
            F = D_CHUNKS[cc]
            codes_src = d_codes[rs * 128:(rs + 1) * 128, D_COFF[cc]:D_COFF[cc] + F]
            am_src = d_absmax[rs * 128:(rs + 1) * 128,
                              D_COFF[cc] // 64:(D_COFF[cc] + F) // 64]
            wt_dst = wdT_sc[D_COFF[cc] // 128:(D_COFF[cc] + F) // 128, rs]
        nb = F // 64
        e = ti % 2
        cw = []
        if ti >= 2:
            cw.append(("wv", prev_wv[ti - 2]))
        else:
            cw.append(("we", LN1_WE))
        dma(dq_codes[e][:, 0:F], codes_src, inc="cd", wait=cw)
        cdv = dma(amt[:, e * 32:e * 32 + nb], am_src, inc="cd")

        gq_base = cur("gq")

        def v_seed(E, e=e, F=F):
            return E.tensor_scalar(out=acc[e][:, 0:F], in0=dq_codes[e][:, 0:F],
                                   scalar1=1.0, scalar2=float(DELT[0]),
                                   op0=AluOpType.is_ge, op1=AluOpType.mult)
        op("vector", v_seed, inc="dv", wait=[("cd", cdv)])

        for di in range(1, GP_FIRST):
            def v_term(E, e=e, F=F, di=di):
                E.tensor_scalar(out=term[:, 0:F], in0=dq_codes[e][:, 0:F],
                                scalar1=float(di + 1), scalar2=float(DELT[di]),
                                op0=AluOpType.is_ge, op1=AluOpType.mult)
                return E.tensor_add(acc[e][:, 0:F], acc[e][:, 0:F], term[:, 0:F])
            op("vector", v_term, inc="dv")

        gp_targets = []
        for gi, di in enumerate(range(GP_FIRST, 15)):
            gb = gterm[gi % 2]
            wgt = []
            if gi >= 2:
                wgt.append(("gq", gq_base + gi - 1))
            elif ti >= 1:
                last_gi = n_gp - 1 - ((n_gp - 1 - (gi % 2)) % 2)
                wgt.append(("gq", gq_base - (n_gp - 1 - last_gi)))

            def v_gterm(E, e=e, F=F, di=di, gb=gb):
                return E.tensor_scalar(out=gb[:, 0:F], in0=dq_codes[e][:, 0:F],
                                       scalar1=float(di + 1), scalar2=float(DELT[di]),
                                       op0=AluOpType.is_ge, op1=AluOpType.mult)
            gp_targets.append(op("vector", v_gterm, inc="dq", wait=wgt))

        for gi in range(n_gp):
            gb = gterm[gi % 2]
            if gi == 0:
                op("gpsimd", lambda E, gb=gb, F=F, e=e: E.tensor_copy(
                    gacc[e][:, 0:F], gb[:, 0:F]),
                   inc="gq", wait=[("dq", gp_targets[0])])
            else:
                op("gpsimd", lambda E, gb=gb, F=F, e=e: E.tensor_add(
                    gacc[e][:, 0:F], gacc[e][:, 0:F], gb[:, 0:F]),
                   inc="gq", wait=[("dq", gp_targets[gi])])
        op("gpsimd", lambda E: E.engine_nop(), inc="gp")
        gpv = cur("gp")

        def mk_part2(ti=ti, tile=tile, e=e, F=F, nb=nb, wt_dst=wt_dst,
                     gpv=gpv, kind=kind):
            def emit():
                def v_merge(E):
                    return E.tensor_add(acc[e][:, 0:F], acc[e][:, 0:F],
                                        gacc[e][:, 0:F])
                mw = [("gp", gpv)]
                if ti >= 2:
                    mw.append(("ws", deq_done[ti - 2]))
                op("vector", v_merge, inc="dv", wait=mw)

                def v_w(E):
                    am_b = amt.ap()[:, e * 32:e * 32 + nb].unsqueeze(2).broadcast_to((128, nb, 64))
                    return E.scalar_tensor_tensor(
                        out=dq_w[e][:, 0:F].rearrange("p (b k) -> p b k", b=nb),
                        in0=acc[e][:, 0:F].rearrange("p (b k) -> p b k", b=nb),
                        scalar=float(NF4[0]), in1=am_b,
                        op0=AluOpType.add, op1=AluOpType.mult)
                prev_wv[ti] = op("vector", v_w, inc="wv")

                ev = None
                for k in range(F // 128):
                    ev = transpose_block(
                        dq_w[e][:, k * 128:(k + 1) * 128],
                        dq_w[e].rearrange("p (k c) -> p k c", c=128)[:, k, :],
                        identb[:],
                        wait_t=[("wv", prev_wv[ti])] if k == 0 else None)
                deq_done[ti] = dma(wt_dst.rearrange("k p c -> p k c"),
                                   dq_w[e][:, 0:F].rearrange("p (k c) -> p k c", c=128),
                                   inc="ws", wait=[("we", ev)])
                if kind in ("g", "u") and tile[2] == 1:
                    gu_ws[(kind, tile[1])] = deq_done[ti]
            return emit
        part2[ti] = mk_part2()
        if ti >= 1:
            part2[ti - 1]()
    part2[len(deq_tiles) - 1]()
    DEQ_WS = cur("ws")

    def t_la(E):
        last = None
        for m in range(2):
            for k in range(DMC):
                lhs = lora_a[:, (m * DMC + k) * 16:(m * DMC + k) * 16 + 16]
                for tg in range(2):
                    last = E.matmul(ps_all[0:16, 2048 + m * 1024 + tg * 512:
                                           2048 + m * 1024 + (tg + 1) * 512],
                                    lhs,
                                    x1T[:, k * 1024 + tg * 512:k * 1024 + (tg + 1) * 512],
                                    start=(k == 0), stop=(k == DMC - 1))
        return last
    lav = op("tensor", t_la, inc="pe",
             wait=[("we", cur("we")), ("dv", cur("dv")), ("ws", cur("ws"))])
    lev = op("scalar", lambda E: E.copy(tmp_l[:, :], ps_all[0:16, 2048:4096]),
             inc="ev", wait=[("pe", lav)])

    W3 = [A_bf[:, 0:4096], A_bf[:, 4096:8192], A_bf[:, 8192:12288]]
    w3_user = [0, 0, 0]
    r1 = A_f2_b[:, 0:1024]
    x3t = A_f2_b[:, 1024:2048]
    p5_mm = {}
    p5_x3 = {}
    for s in range(NS):
        bg = (2 * s) % 3
        bu = (2 * s + 1) % 3
        wldg = dma(W3[bg].rearrange("p (k c) -> p k c", c=128),
                   wgT_sc[s].rearrange("k p c -> p k c"),
                   wait=[("ws", gu_ws[("g", s)] if s > 0 else cur("ws"))] +
                        ([("pe", w3_user[bg])] if w3_user[bg] else []))
        wldu = dma(W3[bu].rearrange("p (k c) -> p k c", c=128),
                   wuT_sc[s].rearrange("k p c -> p k c"),
                   wait=[("ws", gu_ws[("u", s)])] +
                        ([("pe", w3_user[bu])] if w3_user[bu] else []))
        lbl = dma(lbf[s % 2][:, :], g_lb[:, s * 128:(s + 1) * 128],
                  wait=[("dv", p5_x3[s - 2])] if s >= 2 else [])
        lbl2 = dma(dlb_f[:, 0:128], u_lb[:, s * 128:(s + 1) * 128])
        lbc = op("vector", lambda E, s=s: (E.tensor_copy(lbs[s % 2][:, :], lbf[s % 2][:, :]),
                                           E.tensor_copy(dlb_t[:, 0:128], dlb_f[:, 0:128]))[-1],
                 inc="dv", wait=[("ld", lbl2)])

        def t_gu(E, s=s, bg=bg, bu=bu):
            last = None
            for m, bb in ((0, bg), (1, bu)):
                pb = m * 2
                for k in range(DMC):
                    lhs = W3[bb][:, k * 128:(k + 1) * 128]
                    for tg in range(2):
                        last = E.matmul(bank(pb + tg), lhs,
                                        x1T[:, k * 1024 + tg * 512:k * 1024 + (tg + 1) * 512],
                                        start=(k == 0), stop=False)
                lb = lbs[s % 2] if m == 0 else dlb_t[:, 0:128]
                for tg in range(2):
                    last = E.matmul(bank(pb + tg), lb[0:16, :],
                                    tmp_l[:, m * 1024 + tg * 512:m * 1024 + (tg + 1) * 512],
                                    start=False, stop=True)
            return last
        wmm = [("ld", wldu), ("dv", lbc), ("ev", lev)]
        if s >= 1:
            wmm.append(("dv", p5_x3[s - 1]))
        p5_mm[s] = op("tensor", t_gu, inc="pe", wait=wmm)
        w3_user[bg] = p5_mm[s]
        w3_user[bu] = p5_mm[s]

        rv = op("scalar", lambda E: (E.activation(out=r1.rearrange("p (t c) -> p t c", c=512)[:, 0, :],
                                                  in_=bank(0), func=AF.Relu),
                                     E.activation(out=r1.rearrange("p (t c) -> p t c", c=512)[:, 1, :],
                                                  in_=bank(1), func=AF.Relu))[-1],
                inc="ev", wait=[("pe", p5_mm[s])])

        def v_x3(E):
            E.tensor_mul(x3t[:, 0:512], r1[:, 0:512], bank(2).bitcast(F32) if False else bank(2))
            return E.tensor_mul(x3t[:, 512:1024], r1[:, 512:1024], bank(3))
        x3w = [("ev", rv)]
        if s >= 1:
            x3w.append(("st", cur("st")))
        p5_x3[s] = op("vector", v_x3, inc="dv", wait=x3w)

        def t_dl(E, s=s):
            lhs = lora_a[:, (64 + s) * 16:(64 + s) * 16 + 16]
            E.matmul(ps_all[0:16, 3072:3584], lhs, x3t[:, 0:512],
                     start=True, stop=True)
            return E.matmul(ps_all[0:16, 3584:4096], lhs, x3t[:, 512:1024],
                            start=True, stop=True)
        dlv = op("tensor", t_dl, inc="pe", wait=[("dv", p5_x3[s])])
        op("vector", lambda E: E.scalar_tensor_tensor(
            out=tmp_d[:, :], in0=ps_all[0:16, 3072:4096], scalar=1.0,
            in1=tmp_d[:, :], op0=AluOpType.mult, op1=AluOpType.add),
           inc="dv", wait=[("pe", dlv)])
        dma(x3_sc[:, s].rearrange("t p c -> p t c"),
            x3t.rearrange("p (t c) -> p t c", c=128), inc="st",
            wait=[("dv", cur("dv"))])
    P5_ST = cur("st")
    tdv = op("vector", lambda E: E.tensor_copy(tmp_db[:, :], tmp_d[:, :]),
             inc="dv", wait=[("dv", cur("dv"))])

    wdw = A_bf[:, 0:NS * 512]
    x3l = A_f2_b[:, 0:NS * 128]
    y3 = [A_f1[:, 0:512], A_f1[:, 512:1024]]
    x1t6 = [A_f1[:, 1024:1536], A_f1[:, 1536:2048]]
    sqd = A_f1[:, 2048:2560]
    outt = A_f1[:, 2560:3072]
    zset = op("vector", lambda E: (E.memset(st_sum[:, :], 0.0),
                                   E.memset(st_sq[:, :], 0.0))[-1],
              inc="dv", wait=[])
    p6_mm = {}
    p6_ev = {}
    for dp in range(8):
        wload = []
        if dp >= 1:
            wload.append(("pe", p6_mm[(dp - 1, NT - 1)]))
        wl = None
        for sblk in range(NS):
            wl = dma(wdw[:, sblk * 512:(sblk + 1) * 512].rearrange("p (m c) -> p m c", c=128),
                     wdT_sc[sblk, dp * 4:dp * 4 + 4].rearrange("m p c -> p m c"),
                     wait=(wload + [("ws", DEQ_WS)]) if sblk == 0 else [])
        dl = dma(dlb_f[:, 0:512], d_lb[:, dp * 512:(dp + 1) * 512])
        dlc = op("vector", lambda E: E.tensor_copy(dlb_t[:, 0:512], dlb_f[:, 0:512]),
                 inc="dv", wait=[("ld", dl), ("dv", tdv)])
        for tt in range(NT):
            xw = [("st", P5_ST)]
            if tt >= 1:
                xw.append(("pe", p6_mm[(dp, tt - 1)]))
            elif dp >= 1:
                xw.append(("pe", p6_mm[(dp - 1, NT - 1)]))
            xl = dma(x3l[:, 0:NS * 128].rearrange("p (s c) -> p s c", c=128),
                     x3_sc[tt].rearrange("s p c -> p s c"), wait=xw)

            def t_dp(E, tt=tt, dp=dp):
                pb = tt % 2
                last = None
                for sblk in range(NS):
                    last = E.matmul(bank(pb), x3l[:, sblk * 128:(sblk + 1) * 128],
                                    wdw[:, sblk * 512:(sblk + 1) * 512],
                                    start=(sblk == 0), stop=False)
                return E.matmul(bank(pb), tmp_db[:, tt * 128:(tt + 1) * 128],
                                dlb_t[0:16, 0:512], start=False, stop=True)
            wmm = [("ld", max(wl, xl)), ("dv", dlc)]
            if (dp, tt - 2) in p6_ev:
                wmm.append(("dv", p6_ev[(dp, tt - 2)]))
            p6_mm[(dp, tt)] = op("tensor", t_dp, inc="pe", wait=wmm)

            x1l = dma(x1t6[tt % 2], x1_sc[tt * 128:(tt + 1) * 128, dp * 512:(dp + 1) * 512],
                      wait=[("dv", p6_ev[(dp, tt - 2)])] if (dp, tt - 2) in p6_ev
                      else [("dv", cur("dv"))])

            def v_y3(E, tt=tt, dp=dp):
                pb = tt % 2
                return E.tensor_add(y3[tt % 2], bank(pb), x1t6[tt % 2])
            w6 = [("pe", p6_mm[(dp, tt)]), ("ld", x1l), ("dv", zset),
                  ("st", cur("st"))]
            yv = op("vector", v_y3, inc="dv", wait=w6)
            p6_ev[(dp, tt)] = yv
            dma(res2_sc[tt * 128:(tt + 1) * 128, dp * 512:(dp + 1) * 512],
                y3[tt % 2], inc="st", wait=[("dv", yv)])
    P6_ST = cur("st")

    ln_a2 = A_f2[:, 0:4096]
    cent2 = A_f1[:, 2048:6144]
    onorm = A_f1[:, 6144:10240]
    for tt in range(NT):
        w = [("st", P6_ST)] if tt == 0 else [("dv", cur("dv"))]
        la = dma(ln_a2, res2_sc[tt * 128:(tt + 1) * 128, :], wait=w)
        sv0 = op("scalar", lambda E: E.activation(out=cent2, in_=ln_a2,
                                                  func=AF.Copy,
                                                  accum_out=stat[:, 0:1]),
                 inc="ev", wait=[("ld", la)])

        dv2s = op("vector", lambda E: E.tensor_scalar(
            out=stat[:, 1:2], in0=stat[:, 0:1], scalar1=float(1.0 / DM),
            scalar2=None, op0=AluOpType.mult), inc="dv", wait=[("ev", sv0)])
        gdv2 = op("gpsimd", lambda E: E.tensor_copy(stat[:, 6:7], stat[:, 1:2]),
                 inc="gg", wait=[("dv", dv2s)])
        dv2 = op("vector", lambda E: E.tensor_sub(
            cent2, ln_a2, stat.ap()[:, 6:7].broadcast_to((128, DM))),
            inc="dv", wait=[("gg", gdv2)])
        sq2 = op("scalar", lambda E: E.activation(out=ln_a2, in_=cent2,
                                                  func=AF.Square,
                                                  accum_out=stat[:, 2:3]),
                 inc="ev", wait=[("dv", dv2)])
        vv2 = op("vector", lambda E: E.tensor_scalar(
            out=stat[:, 3:4], in0=stat[:, 2:3], scalar1=float(1.0 / DM),
            scalar2=float(EPS), op0=AluOpType.mult, op1=AluOpType.add),
            inc="dv", wait=[("ev", sq2)])

        def s_rsq2(E):
            E.activation(out=stat[:, 4:5], in_=stat[:, 3:4], func=AF.Ln)
            return E.activation(out=stat[:, 5:6], in_=stat[:, 4:5],
                                func=AF.Exp, scale=-0.5)
        ssv2 = op("scalar", s_rsq2, inc="ev", wait=[("dv", vv2)])

        def v_o2(E):
            return E.tensor_mul(onorm, cent2,
                                stat.ap()[:, 5:6].broadcast_to((128, DM)))
        wv2 = [("ev", ssv2)]
        if tt >= 1:
            wv2.append(("ot", cur("ot")))
        nv2 = op("vector", v_o2, inc="dv", wait=wv2)
        dma(out[tt * 128:(tt + 1) * 128, :], onorm, inc="ot",
            wait=[("dv", nv2)])
    OT_ALL = cur("ot")
    op("gpsimd", lambda E: E.engine_nop(), inc="gg", wait=[("ot", OT_ALL)])
    if DEBUG_TAPS:
        dbgmr = sbuf("dbgmr", [128, 16], F32)
        d_mr = dram("dbg_mr", [128, 16], F32, "ExternalOutput")
        d_stats = dram("dbg_stats", [128, 16], F32, "ExternalOutput")
        dma(d_mr, dbgmr[:, :], inc="ot", wait=[("gg", cur("gg"))])
        dma(d_stats[:, 0:8], st_sum[:, :], inc="ot")
        dma(d_stats[:, 8:16], st_sq[:, :], inc="ot")
        d_attn = dram("dbg_attn", [OWN, DM], F32, "ExternalOutput")
        d_x1 = dram("dbg_x1", [OWN, DM], F32, "ExternalOutput")
        d_res2 = dram("dbg_res2", [OWN, DM], F32, "ExternalOutput")
        d_wg = dram("dbg_wg", [DMC, 128, 128], BF16, "ExternalOutput")
        d_wd = dram("dbg_wd", [4, 128, 128], BF16, "ExternalOutput")
        d_x3 = dram("dbg_x3", [NS, 128, 128], BF16, "ExternalOutput")
        d_k0 = dram("dbg_k0", [NTS, 128, 128], BF16, "ExternalOutput")
        d_ctx = dram("dbg_ctx", [NT, 128, 128], BF16, "ExternalOutput")
        d_xt = dram("dbg_xt", [DMC, 128, 128], BF16, "ExternalOutput")
        pairs = [(d_attn, attn_sc), (d_x1, x1_sc), (d_res2, res2_sc),
                 (d_wg, wgT_sc[3]), (d_wd, wdT_sc[5, 0:4]), (d_x3, x3_sc[2]),
                 (d_k0, qkvT_sc[32]), (d_ctx, ctxT_dbg if False else qkvT_sc[0, 0:8]),
                 (d_xt, xT_sc[1])]
        w = [("gg", cur("gg"))]
        for dd, ss in pairs:
            dma(dd, ss, inc="ot", wait=w)
            w = None
        op("gpsimd", lambda E: E.engine_nop(), inc="gg", wait=[("ot", cur("ot"))])

    es.enter_context(nc.allow_non_contiguous_dma(
        reason="small strided bias/lora loads; bulk transfers are contiguous"))
    blk = es.enter_context(nc.Block())
    for ename in ENGS:
        thunks = streams[ename]
        getattr(blk, ename)(lambda E, thunks=thunks: [f(E) for f in thunks])
    es.close()
    return nc


_CACHE = {}


def kernel(L0_attn_hidden_states, L0_attn_qkv_w, L0_attn_qkv_b, L0_attn_out_w,
           L0_attn_out_b, L0_ffn_g_codes, L0_ffn_g_absmax, L0_ffn_g_lora_a,
           L0_ffn_g_lora_b, L0_ffn_u_codes, L0_ffn_u_absmax, L0_ffn_u_lora_a,
           L0_ffn_u_lora_b, L0_ffn_d_codes, L0_ffn_d_absmax, L0_ffn_d_lora_a,
           L0_ffn_d_lora_b):
    x = np.asarray(L0_attn_hidden_states, dtype=np.float32)
    B = x.shape[0]
    common = {
        "qkv_w": np.ascontiguousarray(np.asarray(L0_attn_qkv_w, dtype=np.float32)),
        "qkv_b": np.ascontiguousarray(np.asarray(L0_attn_qkv_b, dtype=np.float32)),
        "out_w": np.ascontiguousarray(np.asarray(L0_attn_out_w, dtype=np.float32)),
        "out_b": np.ascontiguousarray(np.asarray(L0_attn_out_b, dtype=np.float32)),
        "g_codes": np.ascontiguousarray(np.asarray(L0_ffn_g_codes, dtype=np.int32)),
        "g_absmax": np.ascontiguousarray(np.asarray(L0_ffn_g_absmax, dtype=np.float32)),
        "u_codes": np.ascontiguousarray(np.asarray(L0_ffn_u_codes, dtype=np.int32)),
        "u_absmax": np.ascontiguousarray(np.asarray(L0_ffn_u_absmax, dtype=np.float32)),
        "d_codes": np.ascontiguousarray(np.asarray(L0_ffn_d_codes, dtype=np.int32)),
        "d_absmax": np.ascontiguousarray(np.asarray(L0_ffn_d_absmax, dtype=np.float32)),
        "g_la": np.ascontiguousarray(np.asarray(L0_ffn_g_lora_a, dtype=np.float32)),
        "g_lb": np.ascontiguousarray(np.asarray(L0_ffn_g_lora_b, dtype=np.float32)),
        "u_la": np.ascontiguousarray(np.asarray(L0_ffn_u_lora_a, dtype=np.float32)),
        "u_lb": np.ascontiguousarray(np.asarray(L0_ffn_u_lora_b, dtype=np.float32)),
        "d_la": np.ascontiguousarray(np.asarray(L0_ffn_d_lora_a, dtype=np.float32)),
        "d_lb": np.ascontiguousarray(np.asarray(L0_ffn_d_lora_b, dtype=np.float32)),
        "ones_row": np.ones((1, 512), dtype=BF16NP),
        "jcol": (np.arange(128, dtype=np.float32) % 64)[:, None],
    }
    in_maps = []
    for core in range(8):
        b, half = core // 2, core % 2
        own = np.arange(half * OWN, (half + 1) * OWN)
        other = np.arange((1 - half) * OWN, (2 - half) * OWN)
        order = np.concatenate([own, other])
        m = dict(common)
        m["hs"] = np.ascontiguousarray(x[b][order])
        m["kpos"] = np.ascontiguousarray(
            np.broadcast_to(order.astype(np.float32)[None, :], (128, S)))
        m["cmask"] = np.full((1, 128), MASKV if half == 0 else 0.0, dtype=BF16NP)
        in_maps.append(m)

    nc = build_program()
    res = run_bass_kernel_spmd(nc, in_maps, core_ids=list(range(8)))
    outp = np.empty((B, S, DM), dtype=np.float32)
    for core in range(8):
        b, half = core // 2, core % 2
        outp[b, half * OWN:(half + 1) * OWN, :] = res.results[core]["out"]
    return outp


# version 06
# speedup vs baseline: 1.0021x; 1.0021x over previous
import numpy as np
import ml_dtypes
from contextlib import ExitStack

import concourse.bass as bass
import concourse.mybir as mybir
from concourse.bass_utils import run_bass_kernel_spmd
from concourse.alu_op_type import AluOpType

F32 = mybir.dt.float32
BF16 = mybir.dt.bfloat16
I32 = mybir.dt.int32
AF = mybir.ActivationFunctionType
AX = mybir.AxisListType
BF16NP = ml_dtypes.bfloat16

DM, DMC = 4096, 32
S, OWN, NT, NTS = 2048, 1024, 8, 16
NH = 32
DFF, NS = 11008, 86
R = 16
EPS = 1e-5
MASKV = -66504.0
SCALE = 1.0 / float(np.sqrt(np.float64(128.0)))

NF4 = np.array([-1.0, -0.6961928009986877, -0.5250730514526367,
                -0.39491748809814453, -0.28444138169288635,
                -0.18477343022823334, -0.09105003625154495, 0.0,
                0.07958029955625534, 0.16093020141124725, 0.2461123019456863,
                0.33791524171829224, 0.44070982933044434, 0.5626170039176941,
                0.7229568362236023, 1.0], dtype=np.float32)
DELT = np.diff(NF4.astype(np.float64)).astype(np.float32)
GP_FIRST = 7
GU_F = 2048
D_CHUNKS = [1408] * 7 + [1152]
D_COFF = [sum(D_CHUNKS[:i]) for i in range(8)]

ENGS = ("sync", "tensor", "vector", "scalar", "gpsimd")
import os as _os
DEBUG_TAPS = bool(_os.environ.get("KDBG"))


def build_program():
    nc = bass.Bass("TRN2", target_bir_lowering=False, debug=False)

    def dram(name, shape, dt, kind=None):
        if kind:
            return nc.dram_tensor(name, shape, dt, kind=kind).ap()
        return nc.dram_tensor(name, shape, dt).ap()

    hs = dram("hs", [S, DM], F32, "ExternalInput")
    qkv_w = dram("qkv_w", [DM, 3 * DM], F32, "ExternalInput")
    qkv_b = dram("qkv_b", [3 * DM], F32, "ExternalInput")
    out_w = dram("out_w", [DM, DM], F32, "ExternalInput")
    out_b = dram("out_b", [DM], F32, "ExternalInput")
    g_codes = dram("g_codes", [DFF, DM], I32, "ExternalInput")
    g_absmax = dram("g_absmax", [DFF, DM // 64], F32, "ExternalInput")
    u_codes = dram("u_codes", [DFF, DM], I32, "ExternalInput")
    u_absmax = dram("u_absmax", [DFF, DM // 64], F32, "ExternalInput")
    d_codes = dram("d_codes", [DM, DFF], I32, "ExternalInput")
    d_absmax = dram("d_absmax", [DM, DFF // 64], F32, "ExternalInput")
    g_la = dram("g_la", [DM, R], F32, "ExternalInput")
    g_lb = dram("g_lb", [R, DFF], F32, "ExternalInput")
    u_la = dram("u_la", [DM, R], F32, "ExternalInput")
    u_lb = dram("u_lb", [R, DFF], F32, "ExternalInput")
    d_la = dram("d_la", [DFF, R], F32, "ExternalInput")
    d_lb = dram("d_lb", [R, DM], F32, "ExternalInput")
    kpos = dram("kpos", [128, S], F32, "ExternalInput")
    jcol = dram("jcol", [128, 1], F32, "ExternalInput")
    cmask = dram("cmask", [1, 128], BF16, "ExternalInput")
    ones_row = dram("ones_row", [1, 512], BF16, "ExternalInput")
    out = dram("out", [OWN, DM], F32, "ExternalOutput")

    xT_sc = dram("xT_sc", [NTS, DMC, 128, 128], BF16)
    qkvT_sc = dram("qkvT_sc", [96, NTS, 128, 128], BF16)
    wgT_sc = dram("wgT_sc", [NS, DMC, 128, 128], BF16)
    wuT_sc = dram("wuT_sc", [NS, DMC, 128, 128], BF16)
    wdT_sc = dram("wdT_sc", [NS, DMC, 128, 128], BF16)
    x3_sc = dram("x3_sc", [NT, NS, 128, 128], BF16)
    qkvw_bf = dram("qkvw_bf", [96, DMC, 128, 128], BF16)
    attn_sc = dram("attn_sc", [OWN, DM], F32)
    x1_sc = dram("x1_sc", [OWN, DM], F32)
    res2_sc = dram("res2_sc", [OWN, DM], F32)

    es = ExitStack()

    def sbuf(name, shape, dt):
        return es.enter_context(nc.sbuf_tensor(name, shape, dt))

    identb = sbuf("identb", [128, 128], BF16)
    identf = sbuf("identf", [128, 128], F32)
    triag = sbuf("triag", [128, 128], F32)
    invf = sbuf("invf", [128, 1], F32)
    biasq = sbuf("biasq", [128, 96], F32)
    biaso = sbuf("biaso", [128, 32], F32)
    onesb = sbuf("onesb", [1, 512], BF16)
    cmb = sbuf("cmb", [1, 128], BF16)
    amt = sbuf("amt", [128, 64], F32)
    lora_a = sbuf("lora_a", [128, 16 * 150], BF16)
    lbs = [sbuf(f"lbs{i}", [16, 128], BF16) for i in range(2)]
    lbf = [sbuf(f"lbf{i}", [16, 128], F32) for i in range(2)]
    tmp_l = sbuf("tmp_l", [16, 2 * OWN], BF16)
    tmp_d = sbuf("tmp_d", [16, OWN], F32)
    tmp_db = sbuf("tmp_db", [16, OWN], BF16)
    dlb_t = sbuf("dlb_t", [16, 512], BF16)
    dlb_f = sbuf("dlb_f", [16, 512], F32)
    stat = sbuf("stat", [128, 16], F32)
    st_sum = sbuf("st_sum", [128, NT], F32)
    st_sq = sbuf("st_sq", [128, NT], F32)

    A_bf = sbuf("A_bf", [128, 49152], BF16)
    A_f1 = sbuf("A_f1", [128, 12288], F32)
    A_f2 = sbuf("A_f2", [128, 8192], F32)
    A_bf_i = A_bf.ap().bitcast(I32)
    A_f2_b = A_f2.ap().bitcast(BF16)
    lora_as = A_f1[:, 4096:4096 + 16 * NS]
    swp = A_f2_b[:, 0:S + OWN]

    ps_all = es.enter_context(nc.psum_tensor("ps_all", [128, 4096], F32))

    def bank(b, w=1):
        return ps_all[:, b * 512:(b + w) * 512]

    streams = {e: [] for e in ENGS}
    sems = {}

    class _S:
        __slots__ = ("h", "n")

        def __init__(self, h):
            self.h = h
            self.n = 0

    def sem(name):
        if name not in sems:
            sems[name] = _S(es.enter_context(nc.semaphore(name)))
        return sems[name]

    def op(eng, emit, inc=None, k=1, wait=None):
        waits = []
        for w in (wait or []):
            nm, val = w
            if val > 0:
                waits.append((sem(nm).h, val))
        inch = None
        val_out = None
        if inc is not None:
            s = sem(inc)
            s.n += k
            val_out = s.n
            inch = s.h

        def thunk(E, emit=emit, waits=waits, inch=inch, k=k):
            for hh, vv in waits:
                E.wait_ge(hh, vv)
            inst = emit(E)
            if inch is not None:
                inst.then_inc(inch, k)
        streams[eng].append(thunk)
        return val_out

    def dma(outap, inap, inc="ld", wait=None, eng="sync"):
        return op(eng, lambda E: E.dma_start(out=outap, in_=inap), inc=inc,
                  k=16, wait=wait)

    def cur(name):
        return sem(name).n

    trc = [0]

    def transpose_block(src_ap, dst_ap, ident, wait_t=None, wait_s=None):
        i = trc[0]
        b = 4 + (i % 4)
        if src_ap.dtype == BF16:
            pv = bank(b).bitcast(BF16)[:, 0:128]
        else:
            pv = bank(b)[:, 0:128]
        wt = list(wait_t or [])
        if i >= 4:
            wt.append(("we", i - 3))
        op("tensor", lambda E, s=src_ap, idn=ident, pv=pv: E.transpose(pv, s, idn),
           inc="wt", wait=wt)
        ws = list(wait_s or [])
        ws.append(("wt", i + 1))
        ev = op("scalar", lambda E, d=dst_ap, pv=pv: E.copy(d, pv),
                inc="we", wait=ws)
        trc[0] += 1
        return ev

    def g_consts(E):
        E.memset(identb[:], 0.0)
        E.affine_select(out=identb[:], in_=identb[:], compare_op=AluOpType.not_equal,
                        fill=1.0, base=0, pattern=[[-1, 128]], channel_multiplier=1)
        E.memset(identf[:], 0.0)
        E.affine_select(out=identf[:], in_=identf[:], compare_op=AluOpType.not_equal,
                        fill=1.0, base=0, pattern=[[-1, 128]], channel_multiplier=1)
        E.memset(triag[:], 0.0)
        E.affine_select(out=triag[:], in_=triag[:], compare_op=AluOpType.is_ge,
                        fill=MASKV, base=0, pattern=[[-1, 128]], channel_multiplier=1)
        E.memset(tmp_d[:], 0.0)
        return E.engine_nop()
    op("gpsimd", g_consts, inc="gg")

    jld = dma(invf[:, 0:1], jcol[:, :])
    op("scalar", lambda E: E.activation(out=invf[:, 0:1], in_=invf[:, 0:1],
                                        func=AF.Exp,
                                        scale=-float(np.log(10000.0) / 64.0)),
       inc="ev", wait=[("ld", jld)])

    dma(biasq[:, :], qkv_b.rearrange("(c p) -> p c", p=128))
    dma(biaso[:, :], out_b.rearrange("(c p) -> p c", p=128))
    dma(onesb[:, :], ones_row[:, :])
    dma(cmb[:, :], cmask[:, :])
    dma(lora_as[:, 0:16 * DMC].rearrange("p (c r) -> p c r", r=16),
        g_la.rearrange("(c p) r -> p c r", p=128))
    dma(lora_as[:, 16 * DMC:32 * DMC].rearrange("p (c r) -> p c r", r=16),
        u_la.rearrange("(c p) r -> p c r", p=128))
    op("vector", lambda E: E.tensor_copy(lora_a[:, 0:16 * 64], lora_as[:, 0:16 * 64]),
       inc="dv", wait=[("ld", cur("ld"))])
    dma(lora_as[:, 0:16 * NS].rearrange("p (c r) -> p c r", r=16),
        d_la.rearrange("(c p) r -> p c r", p=128), wait=[("dv", cur("dv"))])
    op("vector", lambda E: E.tensor_copy(lora_a[:, 16 * 64:16 * 150],
                                         lora_as[:, 0:16 * NS]),
       inc="dv", wait=[("ld", cur("ld"))])

    ktab_c = A_f1[:, 0:S]
    ktab_s = A_f1[:, S:2 * S]
    angt = A_f1[:, 2 * S:3 * S]
    rop1 = A_f1[:, 3 * S:4 * S]
    rop2 = A_f1[:, 4 * S:5 * S]
    kposb = A_f1[:, 5 * S:6 * S]

    kldv = dma(kposb, kpos[:, :])

    def v_ang(E):
        E.tensor_mul(angt, kposb, invf.ap()[:, 0:1].broadcast_to((128, S)))
        inv2pi = float(1.0 / (2 * np.pi))
        E.tensor_scalar(out=rop2, in0=angt, scalar1=inv2pi, scalar2=0.5,
                        op0=AluOpType.mult, op1=AluOpType.add)
        E.tensor_copy(rop1.bitcast(I32), rop2)
        E.tensor_copy(rop1, rop1.bitcast(I32))
        C1 = float(np.float32(2 * np.pi))
        C2 = float(np.float64(2 * np.pi) - np.float64(np.float32(2 * np.pi)))
        E.scalar_tensor_tensor(out=angt, in0=rop1, scalar=-C1, in1=angt,
                               op0=AluOpType.mult, op1=AluOpType.add)
        E.scalar_tensor_tensor(out=angt, in0=rop1, scalar=-C2, in1=angt,
                               op0=AluOpType.mult, op1=AluOpType.add)
        E.tensor_scalar(out=rop2, in0=angt, scalar1=float(np.pi),
                        scalar2=float(-2 * np.pi), op0=AluOpType.is_gt,
                        op1=AluOpType.mult)
        E.tensor_add(angt, angt, rop2)
        E.tensor_scalar(out=rop2, in0=angt, scalar1=float(-np.pi),
                        scalar2=float(2 * np.pi), op0=AluOpType.is_lt,
                        op1=AluOpType.mult)
        return E.tensor_add(angt, angt, rop2)
    op("vector", v_ang, inc="dv", wait=[("ld", kldv), ("ev", cur("ev"))])
    op("scalar", lambda E: E.activation(out=ktab_s, in_=angt, func=AF.Sin),
       inc="ev", wait=[("dv", cur("dv"))])

    def v_cosarg(E):
        E.tensor_scalar(out=rop1, in0=angt, scalar1=float(np.pi / 2), scalar2=None,
                        op0=AluOpType.add)
        E.tensor_scalar(out=rop2, in0=rop1, scalar1=float(np.pi),
                        scalar2=float(-2 * np.pi), op0=AluOpType.is_gt,
                        op1=AluOpType.mult)
        return E.tensor_add(rop1, rop1, rop2)
    op("vector", v_cosarg, inc="dv", wait=[("ev", cur("ev"))])
    op("scalar", lambda E: E.activation(out=ktab_c, in_=rop1, func=AF.Sin),
       inc="ev", wait=[("dv", cur("dv"))])

    xin = [A_f2[:, 0:4096], A_f2[:, 4096:8192]]
    xstg = [A_bf[:, 0:4096], A_bf[:, 4096:8192]]
    p1_st = {}
    p1_tr_end = {}
    for tt in range(NTS):
        w = []
        if tt >= 2:
            w.append(("we", p1_tr_end[tt - 2]))
            w.append(("st", p1_st[tt - 2]))
        ldv = dma(xin[tt % 2], hs[tt * 128:(tt + 1) * 128, :], wait=w)
        ev = None
        for k in range(DMC):
            ev = transpose_block(
                xin[tt % 2][:, k * 128:(k + 1) * 128],
                xstg[tt % 2].rearrange("p (k c) -> p k c", c=128)[:, k, :],
                identf[:],
                wait_t=[("ld", ldv)] if k == 0 else None)
        p1_tr_end[tt] = ev
        p1_st[tt] = dma(xT_sc[tt].rearrange("k p c -> p k c"),
                        xstg[tt % 2].rearrange("p (k c) -> p k c", c=128),
                        inc="st", wait=[("we", ev)])
    ST_P1 = cur("st")

    xTh = A_bf[:, 8192:8192 + DMC * 1024]
    wcol = [A_f2[:, 0:4096], A_f2[:, 4096:8192]]
    wbf = [A_bf[:, 0:4096], A_bf[:, 4096:8192]]
    qstg = [A_bf[:, 40960:40960 + 1024], A_bf[:, 41984:41984 + 1024]]
    ofs_half = [list(range(96)), list(range(32, 96))]
    p2_cast, p2_mm, p2_ev, p2_st = {}, {}, {}, {}
    for half in range(2):
        ofl = ofs_half[half]
        xw = [("st", ST_P1)]
        if half == 1:
            xw.append(("pe", p2_mm[(0, 95)]))
        xldv = None
        for t8 in range(8):
            xldv = dma(
                xTh.rearrange("p (k t c) -> p t k c", k=DMC, c=128)[:, t8],
                xT_sc[half * 8 + t8].rearrange("k p c -> p k c"),
                wait=xw if t8 == 0 else None)
        for j, of in enumerate(ofl):
            key = (half, of)
            wld = []
            if j >= 2:
                wld.append(("pe", p2_mm[(half, ofl[j - 2])]))
            if half == 0:
                wl = dma(wcol[j % 2].rearrange("p (k c) -> p k c", c=128),
                         qkv_w.rearrange("(k p) (o c) -> o k p c", p=128, c=128)[of].rearrange("k p c -> p k c"),
                         wait=wld)
                p2_cast[key] = op(
                    "vector", lambda E, j=j: E.tensor_copy(wbf[j % 2], wcol[j % 2]),
                    inc="dv", wait=[("ld", wl)] + wld)
                dma(qkvw_bf[of].rearrange("k p c -> p k c"),
                    wbf[j % 2].rearrange("p (k c) -> p k c", c=128),
                    inc="st", wait=[("dv", p2_cast[key])])
                wsrc = ("dv", p2_cast[key])
            else:
                wl = dma(wbf[j % 2].rearrange("p (k c) -> p k c", c=128),
                         qkvw_bf[of].rearrange("k p c -> p k c"), wait=wld)
                wsrc = ("ld", wl)

            def t_mm(E, j=j):
                pb = (j % 2) * 2
                last = None
                for k in range(DMC):
                    lhs = wbf[j % 2][:, k * 128:(k + 1) * 128]
                    for tg in range(2):
                        last = E.matmul(bank(pb + tg), lhs,
                                        xTh[:, k * 1024 + tg * 512:k * 1024 + (tg + 1) * 512],
                                        start=(k == 0), stop=(k == DMC - 1))
                return last
            wmm = [("ld", xldv), wsrc]
            if j >= 2:
                wmm.append(("ev", p2_ev[(half, ofl[j - 2])]))
            p2_mm[key] = op("tensor", t_mm, inc="pe", wait=wmm)

            def s_ev_f(E, j=j, of=of):
                pb = (j % 2) * 2
                last = None
                for tg in range(2):
                    last = E.tensor_add(
                        qstg[j % 2].rearrange("p (t c) -> p t c", c=512)[:, tg, :],
                        bank(pb + tg),
                        biasq.ap()[:, of:of + 1].broadcast_to((128, 512)))
                return last
            wev = [("pe", p2_mm[key])]
            if j >= 2:
                wev.append(("st", p2_st[(half, ofl[j - 2])]))
            p2_ev[key] = op("vector", s_ev_f, inc="ev", wait=wev)
            p2_st[key] = dma(qkvT_sc[of, half * 8:half * 8 + 8].rearrange("t p c -> p t c"),
                             qstg[j % 2].rearrange("p (t c) -> p t c", c=128),
                             inc="st", wait=[("ev", p2_ev[key])])
    ST_P2 = cur("st")

    kb = [A_bf[:, 0:2048], A_bf[:, 2048:4096]]
    qb = [A_bf[:, 4096:5120], A_bf[:, 5120:6144]]
    vbt = [A_bf[:, 6144:8192], A_bf[:, 8192:10240]]
    pbuf = A_bf[:, 10240:12288]
    ptb = A_bf[:, 12288:14336]
    ctx_big = A_bf[:, 14336:14336 + NH * 1024]
    rpA = A_f1[:, 2 * S:3 * S]
    rpB = A_f1[:, 3 * S:4 * S]

    p3_rope, p3_cev = {}, {}
    dbg_hold = []
    pv_last = 0
    if DEBUG_TAPS:
        d_krope = dram("dbg_krope", [128, S], BF16, "ExternalOutput")
        d_qrope = dram("dbg_qrope", [128, OWN], BF16, "ExternalOutput")
        d_p00 = dram("dbg_p00", [128, S], BF16, "ExternalOutput")
        d_vt0 = dram("dbg_vt0", [128, S], BF16, "ExternalOutput")
        d_ctx0 = dram("dbg_ctx0", [128, OWN], BF16, "ExternalOutput")
        d_s00 = dram("dbg_s00", [128, S], F32, "ExternalOutput")
    for h in range(NH):
        e = h % 2
        w = [("st", ST_P2)]
        if h >= 2:
            w.append(("pe", pv_last))
            w.append(("ev", p3_cev[h - 2]))
        dma(kb[e].rearrange("p (t c) -> p t c", c=128),
            qkvT_sc[32 + h].rearrange("t p c -> p t c"), wait=w)
        dma(qb[e].rearrange("p (t c) -> p t c", c=128),
            qkvT_sc[h, 0:8].rearrange("t p c -> p t c"))
        ldv = dma(vbt[e].rearrange("p (t c) -> p t c", c=128),
                  qkvT_sc[64 + h].rearrange("t p c -> p t c"))
        sw = [("ld", ldv)]
        if h >= 1:
            sw.append(("dv", p3_rope[h - 1]))
        dma(swp[0:64, 0:S], kb[e][64:128, 0:S], wait=sw)
        dma(swp[64:128, 0:S], kb[e][0:64, 0:S])
        dma(swp[0:64, S:S + OWN], qb[e][64:128, 0:OWN])
        swv = dma(swp[64:128, S:S + OWN], qb[e][0:64, 0:OWN])

        def v_rope(E, e=e):
            E.tensor_mul(rpA[0:64, 0:S], kb[e][0:64, 0:S], ktab_c[0:64, 0:S])
            E.tensor_mul(rpB[0:64, 0:S], swp[0:64, 0:S], ktab_s[0:64, 0:S])
            E.tensor_sub(kb[e][0:64, 0:S], rpA[0:64, 0:S], rpB[0:64, 0:S])
            E.tensor_mul(rpA[64:128, 0:S], kb[e][64:128, 0:S], ktab_c[64:128, 0:S])
            E.tensor_mul(rpB[64:128, 0:S], swp[64:128, 0:S], ktab_s[64:128, 0:S])
            E.tensor_add(kb[e][64:128, 0:S], rpA[64:128, 0:S], rpB[64:128, 0:S])
            E.tensor_mul(rpA[0:64, 0:OWN], qb[e][0:64, 0:OWN], ktab_c[0:64, 0:OWN])
            E.tensor_mul(rpB[0:64, 0:OWN], swp[0:64, S:S + OWN], ktab_s[0:64, 0:OWN])
            E.tensor_sub(qb[e][0:64, 0:OWN], rpA[0:64, 0:OWN], rpB[0:64, 0:OWN])
            E.tensor_mul(rpA[64:128, 0:OWN], qb[e][64:128, 0:OWN], ktab_c[64:128, 0:OWN])
            E.tensor_mul(rpB[64:128, 0:OWN], swp[64:128, S:S + OWN], ktab_s[64:128, 0:OWN])
            return E.tensor_add(qb[e][64:128, 0:OWN], rpA[64:128, 0:OWN],
                                rpB[64:128, 0:OWN])
        p3_rope[h] = op("vector", v_rope, inc="dv", wait=[("ld", swv)])
        if DEBUG_TAPS and h == 0:
            dma(d_krope, kb[e][:, 0:S], inc="st", wait=[("dv", p3_rope[h])])
            dma(d_qrope, qb[e][:, 0:OWN], inc="st", wait=[("dv", p3_rope[h])])

        vev = None
        for j in range(NTS):
            vev = transpose_block(
                vbt[e][:, j * 128:(j + 1) * 128],
                vbt[e].rearrange("p (t c) -> p t c", c=128)[:, j, :],
                identb[:],
                wait_t=[("dv", p3_rope[h])] if j == 0 else None)

        if DEBUG_TAPS and h == 0:
            dma(d_vt0, vbt[e][:, 0:S], inc="st", wait=[("we", vev)])
        for qt in range(NT):
            eo = (qt + 1) * 128

            def t_scores(E, e=e, qt=qt, eo=eo):
                lhs = qb[e][:, qt * 128:(qt + 1) * 128]
                for gi in range((eo + 511) // 512):
                    c0 = gi * 512
                    c1 = min(eo, c0 + 512)
                    E.matmul(ps_all[:, c0:c1], lhs, kb[e][:, c0:c1],
                             start=True, stop=True)
                last = None
                for gi in range(2):
                    c0 = 1024 + gi * 512
                    E.matmul(ps_all[:, c0:c0 + 512], lhs, kb[e][:, c0:c0 + 512],
                             start=True, stop=False)
                    last = E.matmul(ps_all[:, c0:c0 + 512], cmb[:, :], onesb[:, :],
                                    start=False, stop=True)
                return last
            wsc = []
            if qt == 0:
                wsc += [("dv", p3_rope[h]), ("we", vev)]
            else:
                wsc.append(("ev", p3_cev[(h, qt - 1)]))
            wsc += dbg_hold
            dbg_hold.clear() if dbg_hold else None
            mmv = op("tensor", t_scores, inc="pe", wait=wsc)

            mskv = op("vector",
                      lambda E, qt=qt: E.tensor_add(
                          ps_all[:, qt * 128:(qt + 1) * 128],
                          ps_all[:, qt * 128:(qt + 1) * 128], triag[:]),
                      inc="dv", wait=[("pe", mmv)])

            def s_exp(E, eo=eo):
                E.activation(out=rpA[:, 0:eo], in_=ps_all[:, 0:eo], func=AF.Exp,
                             scale=SCALE, accum_out=stat[:, 0:1])
                return E.activation(out=rpA[:, 1024:2048], in_=ps_all[:, 1024:2048],
                                    func=AF.Exp, scale=SCALE, accum_out=stat[:, 1:2])
            expv = op("scalar", s_exp, inc="ev", wait=[("dv", mskv)])

            sumv = op("vector",
                      lambda E: E.tensor_add(stat[:, 2:3], stat[:, 0:1],
                                             stat[:, 1:2]),
                      inc="dv", wait=[("ev", expv)])

            def s_recip(E):
                E.activation(out=stat[:, 4:5], in_=stat[:, 2:3], func=AF.Ln)
                return E.activation(out=stat[:, 3:4], in_=stat[:, 4:5],
                                    func=AF.Exp, scale=-1.0)
            rcpv = op("scalar", s_recip, inc="ev", wait=[("dv", sumv)])

            def v_norm(E, eo=eo):
                E.tensor_mul(pbuf[:, 0:eo], rpA[:, 0:eo],
                             stat.ap()[:, 3:4].broadcast_to((128, eo)))
                return E.tensor_mul(pbuf[:, 1024:2048], rpA[:, 1024:2048],
                                    stat.ap()[:, 3:4].broadcast_to((128, 1024)))
            nrmv = op("vector", v_norm, inc="dv", wait=[("ev", rcpv)])
            if DEBUG_TAPS and h == 0 and qt == 0:
                svev = op("scalar", lambda E: E.copy(rop2, ps_all[:, 0:S]),
                          inc="ev", wait=[("dv", mskv)])
                dbg_hold.append(("st", dma(d_s00, rop2, inc="st",
                                           wait=[("ev", svev)])))
                dbg_hold.append(("st", dma(d_p00, pbuf[:, 0:S], inc="st",
                                           wait=[("dv", nrmv)])))

            cols = [j * 128 for j in range(eo // 128)] + \
                   [1024 + j * 128 for j in range(8)]
            lastev = None
            for j, col in enumerate(cols):
                lastev = transpose_block(
                    pbuf[:, col:col + 128],
                    ptb.rearrange("p (t c) -> p t c", c=128)[:, j, :],
                    identb[:],
                    wait_t=[("dv", nrmv)] if j == 0 else None)

            def t_pv(E, e=e, cols=cols):
                last = None
                for j, col in enumerate(cols):
                    last = E.matmul(bank(0)[:, 0:128], vbt[e][:, col:col + 128],
                                    ptb[0:128, j * 128:(j + 1) * 128],
                                    start=(j == 0), stop=(j == len(cols) - 1))
                return last
            pv_last = op("tensor", t_pv, inc="pe", wait=[("we", lastev)])
            p3_cev[(h, qt)] = op(
                "scalar",
                lambda E, h=h, qt=qt: E.copy(
                    ctx_big[:, h * 1024 + qt * 128:h * 1024 + (qt + 1) * 128],
                    bank(0)[:, 0:128]),
                inc="ev", wait=[("pe", pv_last)])
        if DEBUG_TAPS and h == 0:
            dma(d_ctx0, ctx_big[:, 0:OWN], inc="st",
                wait=[("ev", p3_cev[(0, NT - 1)])])
        p3_cev[h] = p3_cev[(h, NT - 1)]
    P3_EV = cur("ev")

    wcol4 = [A_f2[:, 0:4096], A_f2[:, 4096:8192]]
    wbf4 = [A_bf[:, 0:4096], A_bf[:, 4096:8192]]
    attn_ev = A_f1[:, 0:1024]
    tstg4 = A_f1[:, 1024:1152]
    p4_mm, p4_ev = {}, {}
    p4_blk_st = {}
    for of in range(DMC):
        wld = []
        if of >= 2:
            wld.append(("pe", p4_mm[of - 2]))
        else:
            wld.append(("pe", pv_last))
            wld.append(("ev", P3_EV))
        wl = dma(wcol4[of % 2].rearrange("p (k c) -> p k c", c=128),
                 out_w.rearrange("(k p) (o c) -> o k p c", p=128, c=128)[of].rearrange("k p c -> p k c"),
                 wait=wld)
        cv = op("vector", lambda E, of=of: E.tensor_copy(wbf4[of % 2], wcol4[of % 2]),
                inc="dv", wait=[("ld", wl)] + wld)

        def t_op(E, of=of):
            pb = (of % 2) * 2
            last = None
            for k in range(DMC):
                lhs = wbf4[of % 2][:, k * 128:(k + 1) * 128]
                for tg in range(2):
                    last = E.matmul(bank(pb + tg), lhs,
                                    ctx_big[:, k * 1024 + tg * 512:k * 1024 + (tg + 1) * 512],
                                    start=(k == 0), stop=(k == DMC - 1))
            return last
        wmm = [("dv", cv), ("ev", P3_EV)]
        if of >= 2:
            wmm.append(("ev", p4_ev[of - 2]))
        p4_mm[of] = op("tensor", t_op, inc="pe", wait=wmm)

        def s_ev4(E, of=of):
            pb = (of % 2) * 2
            last = None
            for tg in range(2):
                last = E.tensor_add(
                    attn_ev.rearrange("p (t c) -> p t c", c=512)[:, tg, :],
                    bank(pb + tg),
                    biaso.ap()[:, of:of + 1].broadcast_to((128, 512)))
            return last
        wev = [("pe", p4_mm[of])]
        if of >= 1:
            wev.append(("st", p4_blk_st[of - 1]))
        p4_ev[of] = op("vector", s_ev4, inc="ev", wait=wev)

        for j in range(NT):
            ev = transpose_block(
                attn_ev.rearrange("p (t c) -> p t c", c=128)[:, j, :],
                tstg4, identf[:],
                wait_t=[("ev", p4_ev[of])] if j == 0 else
                       [("st", cur("st"))])
            p4_blk_st[of] = dma(
                attn_sc.rearrange("(t p) (o c) -> o t p c", p=128, c=128)[of, j],
                tstg4, inc="st", wait=[("we", ev)])

    x1T = A_bf[:, 14336:14336 + DMC * 1024]
    x1bf = A_bf[:, 0:4096]
    ln_a = A_f2[:, 0:4096]
    ln_b = A_f2[:, 4096:8192]
    res_t = A_f1[:, 2048:6144]
    cent = A_f1[:, 6144:10240]
    P4_ST_ALL = cur("st")
    for tt in range(NT):
        la = dma(ln_a, attn_sc[tt * 128:(tt + 1) * 128, :],
                 wait=[("st", P4_ST_ALL)] if tt == 0 else [("dv", cur("dv"))])
        lb = dma(ln_b, hs[tt * 128:(tt + 1) * 128, :])

        dv0 = op("vector", lambda E: E.tensor_add(res_t, ln_a, ln_b),
                 inc="dv", wait=[("ld", lb)])
        sv0 = op("scalar", lambda E: E.activation(out=ln_a, in_=res_t, func=AF.Copy,
                                                  accum_out=stat[:, 0:1]),
                 inc="ev", wait=[("dv", dv0)])

        dvs = op("vector", lambda E: E.tensor_scalar(
            out=stat[:, 1:2], in0=stat[:, 0:1], scalar1=float(1.0 / DM),
            scalar2=None, op0=AluOpType.mult), inc="dv", wait=[("ev", sv0)])
        gdv = op("gpsimd", lambda E: E.tensor_copy(stat[:, 6:7], stat[:, 1:2]),
                 inc="gg", wait=[("dv", dvs)])
        dvv = op("vector", lambda E: E.tensor_sub(
            cent, res_t, stat.ap()[:, 6:7].broadcast_to((128, DM))),
            inc="dv", wait=[("gg", gdv)])

        sq = op("scalar", lambda E: E.activation(out=ln_a, in_=cent, func=AF.Square,
                                                 accum_out=stat[:, 2:3]),
                inc="ev", wait=[("dv", dvv)])

        def v_var(E):
            E.tensor_scalar(out=stat[:, 3:4], in0=stat[:, 2:3],
                            scalar1=float(1.0 / DM), scalar2=float(EPS),
                            op0=AluOpType.mult, op1=AluOpType.add)
            return E.engine_nop()
        vv = op("vector", v_var, inc="dv", wait=[("ev", sq)])
        def s_rsq(E):
            E.activation(out=stat[:, 4:5], in_=stat[:, 3:4], func=AF.Ln)
            return E.activation(out=stat[:, 5:6], in_=stat[:, 4:5], func=AF.Exp,
                                scale=-0.5)
        sqv = op("scalar", s_rsq, inc="ev", wait=[("dv", vv)])

        def v_x1(E):
            E.tensor_mul(ln_b, cent, stat.ap()[:, 5:6].broadcast_to((128, DM)))
            return E.tensor_copy(x1bf, ln_b)
        x1v = op("vector", v_x1, inc="dv",
                 wait=[("ev", sqv), ("we", cur("we"))])
        dma(x1_sc[tt * 128:(tt + 1) * 128, :], ln_b, inc="st", wait=[("dv", x1v)])
        for k in range(DMC):
            transpose_block(
                x1bf[:, k * 128:(k + 1) * 128],
                x1T.rearrange("p (k t c) -> p k t c", k=DMC, c=128)[:, k, tt, :],
                identb[:],
                wait_t=[("dv", x1v)] if k == 0 else None)
    LN1_WE = cur("we")

    deq_tiles = []
    for s in range(NS):
        for mat in ("g", "u"):
            for jh in range(2):
                deq_tiles.append((mat, s, jh))
    for rs in range(DMC):
        for cc in range(8):
            deq_tiles.append(("d", rs, cc))

    dq_codes = [A_bf_i[:, 0:2048], A_bf_i[:, 2048:4096]]
    dq_w = [A_bf[:, 8192:10240], A_bf[:, 10240:12288]]
    acc = [A_f1[:, 0:2048], A_f1[:, 2048:4096]]
    term = A_f1[:, 4096:6144]
    gterm = [A_f1[:, 6144:8192], A_f1[:, 8192:10240]]
    gacc = [A_f2[:, 0:2048], A_f2[:, 2048:4096]]

    deq_done = {}
    gu_ws = {}
    prev_wv = {}

    n_gp = 15 - GP_FIRST
    part2 = {}
    for ti, tile in enumerate(deq_tiles):
        kind = tile[0]
        if kind in ("g", "u"):
            _, s, jh = tile
            codes_src = (g_codes if kind == "g" else u_codes)[
                s * 128:(s + 1) * 128, jh * GU_F:(jh + 1) * GU_F]
            am_src = (g_absmax if kind == "g" else u_absmax)[
                s * 128:(s + 1) * 128, jh * (GU_F // 64):(jh + 1) * (GU_F // 64)]
            F = GU_F
            wt_dst = (wgT_sc if kind == "g" else wuT_sc)[s, jh * 16:jh * 16 + 16]
        else:
            _, rs, cc = tile
            F = D_CHUNKS[cc]
            codes_src = d_codes[rs * 128:(rs + 1) * 128, D_COFF[cc]:D_COFF[cc] + F]
            am_src = d_absmax[rs * 128:(rs + 1) * 128,
                              D_COFF[cc] // 64:(D_COFF[cc] + F) // 64]
            wt_dst = wdT_sc[D_COFF[cc] // 128:(D_COFF[cc] + F) // 128, rs]
        nb = F // 64
        e = ti % 2
        cw = []
        if ti >= 2:
            cw.append(("wv", prev_wv[ti - 2]))
        else:
            cw.append(("we", LN1_WE))
        dma(dq_codes[e][:, 0:F], codes_src, inc="cd", wait=cw)
        cdv = dma(amt[:, e * 32:e * 32 + nb], am_src, inc="cd")

        gq_base = cur("gq")

        def v_seed(E, e=e, F=F):
            return E.tensor_scalar(out=acc[e][:, 0:F], in0=dq_codes[e][:, 0:F],
                                   scalar1=1.0, scalar2=float(DELT[0]),
                                   op0=AluOpType.is_ge, op1=AluOpType.mult)
        op("vector", v_seed, inc="dv", wait=[("cd", cdv)])

        gp_targets = []
        for gi, di in enumerate(range(GP_FIRST, 15)):
            gb = gterm[gi % 2]
            wgt = []
            if gi >= 2:
                wgt.append(("gq", gq_base + gi - 1))
            elif ti >= 1:
                last_gi = n_gp - 1 - ((n_gp - 1 - (gi % 2)) % 2)
                wgt.append(("gq", gq_base - (n_gp - 1 - last_gi)))

            def v_gterm(E, e=e, F=F, di=di, gb=gb):
                return E.tensor_scalar(out=gb[:, 0:F], in0=dq_codes[e][:, 0:F],
                                       scalar1=float(di + 1), scalar2=float(DELT[di]),
                                       op0=AluOpType.is_ge, op1=AluOpType.mult)
            gp_targets.append(op("vector", v_gterm, inc="dq", wait=wgt))

            if 1 <= gi < GP_FIRST:
                def v_term(E, e=e, F=F, di=gi):
                    E.tensor_scalar(out=term[:, 0:F], in0=dq_codes[e][:, 0:F],
                                    scalar1=float(di + 1), scalar2=float(DELT[di]),
                                    op0=AluOpType.is_ge, op1=AluOpType.mult)
                    return E.tensor_add(acc[e][:, 0:F], acc[e][:, 0:F],
                                        term[:, 0:F])
                op("vector", v_term, inc="dv")

        for gi in range(n_gp):
            gb = gterm[gi % 2]
            if gi == 0:
                op("gpsimd", lambda E, gb=gb, F=F, e=e: E.tensor_copy(
                    gacc[e][:, 0:F], gb[:, 0:F]),
                   inc="gq", wait=[("dq", gp_targets[0])])
            else:
                op("gpsimd", lambda E, gb=gb, F=F, e=e: E.tensor_add(
                    gacc[e][:, 0:F], gacc[e][:, 0:F], gb[:, 0:F]),
                   inc="gq", wait=[("dq", gp_targets[gi])])
        op("gpsimd", lambda E: E.engine_nop(), inc="gp")
        gpv = cur("gp")

        def mk_part2(ti=ti, tile=tile, e=e, F=F, nb=nb, wt_dst=wt_dst,
                     gpv=gpv, kind=kind):
            def emit():
                def v_merge(E):
                    return E.tensor_add(acc[e][:, 0:F], acc[e][:, 0:F],
                                        gacc[e][:, 0:F])
                mw = [("gp", gpv)]
                if ti >= 2:
                    mw.append(("ws", deq_done[ti - 2]))
                op("vector", v_merge, inc="dv", wait=mw)

                def v_w(E):
                    am_b = amt.ap()[:, e * 32:e * 32 + nb].unsqueeze(2).broadcast_to((128, nb, 64))
                    return E.scalar_tensor_tensor(
                        out=dq_w[e][:, 0:F].rearrange("p (b k) -> p b k", b=nb),
                        in0=acc[e][:, 0:F].rearrange("p (b k) -> p b k", b=nb),
                        scalar=float(NF4[0]), in1=am_b,
                        op0=AluOpType.add, op1=AluOpType.mult)
                prev_wv[ti] = op("vector", v_w, inc="wv")

                ev = None
                for k in range(F // 128):
                    ev = transpose_block(
                        dq_w[e][:, k * 128:(k + 1) * 128],
                        dq_w[e].rearrange("p (k c) -> p k c", c=128)[:, k, :],
                        identb[:],
                        wait_t=[("wv", prev_wv[ti])] if k == 0 else None)
                deq_done[ti] = dma(wt_dst.rearrange("k p c -> p k c"),
                                   dq_w[e][:, 0:F].rearrange("p (k c) -> p k c", c=128),
                                   inc="ws", wait=[("we", ev)])
                if kind in ("g", "u") and tile[2] == 1:
                    gu_ws[(kind, tile[1])] = deq_done[ti]
            return emit
        part2[ti] = mk_part2()
        if ti >= 1:
            part2[ti - 1]()
    part2[len(deq_tiles) - 1]()
    DEQ_WS = cur("ws")

    def t_la(E):
        last = None
        for m in range(2):
            for k in range(DMC):
                lhs = lora_a[:, (m * DMC + k) * 16:(m * DMC + k) * 16 + 16]
                for tg in range(2):
                    last = E.matmul(ps_all[0:16, 2048 + m * 1024 + tg * 512:
                                           2048 + m * 1024 + (tg + 1) * 512],
                                    lhs,
                                    x1T[:, k * 1024 + tg * 512:k * 1024 + (tg + 1) * 512],
                                    start=(k == 0), stop=(k == DMC - 1))
        return last
    lav = op("tensor", t_la, inc="pe",
             wait=[("we", cur("we")), ("dv", cur("dv")), ("ws", cur("ws"))])
    lev = op("scalar", lambda E: E.copy(tmp_l[:, :], ps_all[0:16, 2048:4096]),
             inc="ev", wait=[("pe", lav)])

    W3 = [A_bf[:, 0:4096], A_bf[:, 4096:8192], A_bf[:, 8192:12288]]
    w3_user = [0, 0, 0]
    r1 = A_f2_b[:, 0:1024]
    x3t = A_f2_b[:, 1024:2048]
    p5_mm = {}
    p5_x3 = {}
    for s in range(NS):
        bg = (2 * s) % 3
        bu = (2 * s + 1) % 3
        wldg = dma(W3[bg].rearrange("p (k c) -> p k c", c=128),
                   wgT_sc[s].rearrange("k p c -> p k c"),
                   wait=[("ws", gu_ws[("g", s)] if s > 0 else cur("ws"))] +
                        ([("pe", w3_user[bg])] if w3_user[bg] else []))
        wldu = dma(W3[bu].rearrange("p (k c) -> p k c", c=128),
                   wuT_sc[s].rearrange("k p c -> p k c"),
                   wait=[("ws", gu_ws[("u", s)])] +
                        ([("pe", w3_user[bu])] if w3_user[bu] else []))
        lbl = dma(lbf[s % 2][:, :], g_lb[:, s * 128:(s + 1) * 128],
                  wait=[("dv", p5_x3[s - 2])] if s >= 2 else [])
        lbl2 = dma(dlb_f[:, 0:128], u_lb[:, s * 128:(s + 1) * 128])
        lbc = op("vector", lambda E, s=s: (E.tensor_copy(lbs[s % 2][:, :], lbf[s % 2][:, :]),
                                           E.tensor_copy(dlb_t[:, 0:128], dlb_f[:, 0:128]))[-1],
                 inc="dv", wait=[("ld", lbl2)])

        def t_gu(E, s=s, bg=bg, bu=bu):
            last = None
            for m, bb in ((0, bg), (1, bu)):
                pb = m * 2
                for k in range(DMC):
                    lhs = W3[bb][:, k * 128:(k + 1) * 128]
                    for tg in range(2):
                        last = E.matmul(bank(pb + tg), lhs,
                                        x1T[:, k * 1024 + tg * 512:k * 1024 + (tg + 1) * 512],
                                        start=(k == 0), stop=False)
                lb = lbs[s % 2] if m == 0 else dlb_t[:, 0:128]
                for tg in range(2):
                    last = E.matmul(bank(pb + tg), lb[0:16, :],
                                    tmp_l[:, m * 1024 + tg * 512:m * 1024 + (tg + 1) * 512],
                                    start=False, stop=True)
            return last
        wmm = [("ld", wldu), ("dv", lbc), ("ev", lev)]
        if s >= 1:
            wmm.append(("dv", p5_x3[s - 1]))
        p5_mm[s] = op("tensor", t_gu, inc="pe", wait=wmm)
        w3_user[bg] = p5_mm[s]
        w3_user[bu] = p5_mm[s]

        rv = op("scalar", lambda E: (E.activation(out=r1.rearrange("p (t c) -> p t c", c=512)[:, 0, :],
                                                  in_=bank(0), func=AF.Relu),
                                     E.activation(out=r1.rearrange("p (t c) -> p t c", c=512)[:, 1, :],
                                                  in_=bank(1), func=AF.Relu))[-1],
                inc="ev", wait=[("pe", p5_mm[s])])

        def v_x3(E):
            E.tensor_mul(x3t[:, 0:512], r1[:, 0:512], bank(2).bitcast(F32) if False else bank(2))
            return E.tensor_mul(x3t[:, 512:1024], r1[:, 512:1024], bank(3))
        x3w = [("ev", rv)]
        if s >= 1:
            x3w.append(("st", cur("st")))
        p5_x3[s] = op("vector", v_x3, inc="dv", wait=x3w)

        def t_dl(E, s=s):
            lhs = lora_a[:, (64 + s) * 16:(64 + s) * 16 + 16]
            E.matmul(ps_all[0:16, 3072:3584], lhs, x3t[:, 0:512],
                     start=True, stop=True)
            return E.matmul(ps_all[0:16, 3584:4096], lhs, x3t[:, 512:1024],
                            start=True, stop=True)
        dlv = op("tensor", t_dl, inc="pe", wait=[("dv", p5_x3[s])])
        op("vector", lambda E: E.scalar_tensor_tensor(
            out=tmp_d[:, :], in0=ps_all[0:16, 3072:4096], scalar=1.0,
            in1=tmp_d[:, :], op0=AluOpType.mult, op1=AluOpType.add),
           inc="dv", wait=[("pe", dlv)])
        dma(x3_sc[:, s].rearrange("t p c -> p t c"),
            x3t.rearrange("p (t c) -> p t c", c=128), inc="st",
            wait=[("dv", cur("dv"))])
    P5_ST = cur("st")
    tdv = op("vector", lambda E: E.tensor_copy(tmp_db[:, :], tmp_d[:, :]),
             inc="dv", wait=[("dv", cur("dv"))])

    wdw = A_bf[:, 0:NS * 512]
    x3l = A_f2_b[:, 0:NS * 128]
    y3 = [A_f1[:, 0:512], A_f1[:, 512:1024]]
    x1t6 = [A_f1[:, 1024:1536], A_f1[:, 1536:2048]]
    sqd = A_f1[:, 2048:2560]
    outt = A_f1[:, 2560:3072]
    zset = op("vector", lambda E: (E.memset(st_sum[:, :], 0.0),
                                   E.memset(st_sq[:, :], 0.0))[-1],
              inc="dv", wait=[])
    p6_mm = {}
    p6_ev = {}
    for dp in range(8):
        wload = []
        if dp >= 1:
            wload.append(("pe", p6_mm[(dp - 1, NT - 1)]))
        wl = None
        for sblk in range(NS):
            wl = dma(wdw[:, sblk * 512:(sblk + 1) * 512].rearrange("p (m c) -> p m c", c=128),
                     wdT_sc[sblk, dp * 4:dp * 4 + 4].rearrange("m p c -> p m c"),
                     wait=(wload + [("ws", DEQ_WS)]) if sblk == 0 else [])
        dl = dma(dlb_f[:, 0:512], d_lb[:, dp * 512:(dp + 1) * 512])
        dlc = op("vector", lambda E: E.tensor_copy(dlb_t[:, 0:512], dlb_f[:, 0:512]),
                 inc="dv", wait=[("ld", dl), ("dv", tdv)])
        for tt in range(NT):
            xw = [("st", P5_ST)]
            if tt >= 1:
                xw.append(("pe", p6_mm[(dp, tt - 1)]))
            elif dp >= 1:
                xw.append(("pe", p6_mm[(dp - 1, NT - 1)]))
            xl = dma(x3l[:, 0:NS * 128].rearrange("p (s c) -> p s c", c=128),
                     x3_sc[tt].rearrange("s p c -> p s c"), wait=xw)

            def t_dp(E, tt=tt, dp=dp):
                pb = tt % 2
                last = None
                for sblk in range(NS):
                    last = E.matmul(bank(pb), x3l[:, sblk * 128:(sblk + 1) * 128],
                                    wdw[:, sblk * 512:(sblk + 1) * 512],
                                    start=(sblk == 0), stop=False)
                return E.matmul(bank(pb), tmp_db[:, tt * 128:(tt + 1) * 128],
                                dlb_t[0:16, 0:512], start=False, stop=True)
            wmm = [("ld", max(wl, xl)), ("dv", dlc)]
            if (dp, tt - 2) in p6_ev:
                wmm.append(("dv", p6_ev[(dp, tt - 2)]))
            p6_mm[(dp, tt)] = op("tensor", t_dp, inc="pe", wait=wmm)

            x1l = dma(x1t6[tt % 2], x1_sc[tt * 128:(tt + 1) * 128, dp * 512:(dp + 1) * 512],
                      wait=[("dv", p6_ev[(dp, tt - 2)])] if (dp, tt - 2) in p6_ev
                      else [("dv", cur("dv"))])

            def v_y3(E, tt=tt, dp=dp):
                pb = tt % 2
                return E.tensor_add(y3[tt % 2], bank(pb), x1t6[tt % 2])
            w6 = [("pe", p6_mm[(dp, tt)]), ("ld", x1l), ("dv", zset),
                  ("st", cur("st"))]
            yv = op("vector", v_y3, inc="dv", wait=w6)
            p6_ev[(dp, tt)] = yv
            dma(res2_sc[tt * 128:(tt + 1) * 128, dp * 512:(dp + 1) * 512],
                y3[tt % 2], inc="st", wait=[("dv", yv)])
    P6_ST = cur("st")

    ln_a2 = A_f2[:, 0:4096]
    cent2 = A_f1[:, 2048:6144]
    onorm = A_f1[:, 6144:10240]
    for tt in range(NT):
        w = [("st", P6_ST)] if tt == 0 else [("dv", cur("dv"))]
        la = dma(ln_a2, res2_sc[tt * 128:(tt + 1) * 128, :], wait=w)
        sv0 = op("scalar", lambda E: E.activation(out=cent2, in_=ln_a2,
                                                  func=AF.Copy,
                                                  accum_out=stat[:, 0:1]),
                 inc="ev", wait=[("ld", la)])

        dv2s = op("vector", lambda E: E.tensor_scalar(
            out=stat[:, 1:2], in0=stat[:, 0:1], scalar1=float(1.0 / DM),
            scalar2=None, op0=AluOpType.mult), inc="dv", wait=[("ev", sv0)])
        gdv2 = op("gpsimd", lambda E: E.tensor_copy(stat[:, 6:7], stat[:, 1:2]),
                 inc="gg", wait=[("dv", dv2s)])
        dv2 = op("vector", lambda E: E.tensor_sub(
            cent2, ln_a2, stat.ap()[:, 6:7].broadcast_to((128, DM))),
            inc="dv", wait=[("gg", gdv2)])
        sq2 = op("scalar", lambda E: E.activation(out=ln_a2, in_=cent2,
                                                  func=AF.Square,
                                                  accum_out=stat[:, 2:3]),
                 inc="ev", wait=[("dv", dv2)])
        vv2 = op("vector", lambda E: E.tensor_scalar(
            out=stat[:, 3:4], in0=stat[:, 2:3], scalar1=float(1.0 / DM),
            scalar2=float(EPS), op0=AluOpType.mult, op1=AluOpType.add),
            inc="dv", wait=[("ev", sq2)])

        def s_rsq2(E):
            E.activation(out=stat[:, 4:5], in_=stat[:, 3:4], func=AF.Ln)
            return E.activation(out=stat[:, 5:6], in_=stat[:, 4:5],
                                func=AF.Exp, scale=-0.5)
        ssv2 = op("scalar", s_rsq2, inc="ev", wait=[("dv", vv2)])

        def v_o2(E):
            return E.tensor_mul(onorm, cent2,
                                stat.ap()[:, 5:6].broadcast_to((128, DM)))
        wv2 = [("ev", ssv2)]
        if tt >= 1:
            wv2.append(("ot", cur("ot")))
        nv2 = op("vector", v_o2, inc="dv", wait=wv2)
        dma(out[tt * 128:(tt + 1) * 128, :], onorm, inc="ot",
            wait=[("dv", nv2)])
    OT_ALL = cur("ot")
    op("gpsimd", lambda E: E.engine_nop(), inc="gg", wait=[("ot", OT_ALL)])
    if DEBUG_TAPS:
        dbgmr = sbuf("dbgmr", [128, 16], F32)
        d_mr = dram("dbg_mr", [128, 16], F32, "ExternalOutput")
        d_stats = dram("dbg_stats", [128, 16], F32, "ExternalOutput")
        dma(d_mr, dbgmr[:, :], inc="ot", wait=[("gg", cur("gg"))])
        dma(d_stats[:, 0:8], st_sum[:, :], inc="ot")
        dma(d_stats[:, 8:16], st_sq[:, :], inc="ot")
        d_attn = dram("dbg_attn", [OWN, DM], F32, "ExternalOutput")
        d_x1 = dram("dbg_x1", [OWN, DM], F32, "ExternalOutput")
        d_res2 = dram("dbg_res2", [OWN, DM], F32, "ExternalOutput")
        d_wg = dram("dbg_wg", [DMC, 128, 128], BF16, "ExternalOutput")
        d_wd = dram("dbg_wd", [4, 128, 128], BF16, "ExternalOutput")
        d_x3 = dram("dbg_x3", [NS, 128, 128], BF16, "ExternalOutput")
        d_k0 = dram("dbg_k0", [NTS, 128, 128], BF16, "ExternalOutput")
        d_ctx = dram("dbg_ctx", [NT, 128, 128], BF16, "ExternalOutput")
        d_xt = dram("dbg_xt", [DMC, 128, 128], BF16, "ExternalOutput")
        pairs = [(d_attn, attn_sc), (d_x1, x1_sc), (d_res2, res2_sc),
                 (d_wg, wgT_sc[3]), (d_wd, wdT_sc[5, 0:4]), (d_x3, x3_sc[2]),
                 (d_k0, qkvT_sc[32]), (d_ctx, ctxT_dbg if False else qkvT_sc[0, 0:8]),
                 (d_xt, xT_sc[1])]
        w = [("gg", cur("gg"))]
        for dd, ss in pairs:
            dma(dd, ss, inc="ot", wait=w)
            w = None
        op("gpsimd", lambda E: E.engine_nop(), inc="gg", wait=[("ot", cur("ot"))])

    es.enter_context(nc.allow_non_contiguous_dma(
        reason="small strided bias/lora loads; bulk transfers are contiguous"))
    blk = es.enter_context(nc.Block())
    for ename in ENGS:
        thunks = streams[ename]
        getattr(blk, ename)(lambda E, thunks=thunks: [f(E) for f in thunks])
    es.close()
    return nc


_CACHE = {}


def kernel(L0_attn_hidden_states, L0_attn_qkv_w, L0_attn_qkv_b, L0_attn_out_w,
           L0_attn_out_b, L0_ffn_g_codes, L0_ffn_g_absmax, L0_ffn_g_lora_a,
           L0_ffn_g_lora_b, L0_ffn_u_codes, L0_ffn_u_absmax, L0_ffn_u_lora_a,
           L0_ffn_u_lora_b, L0_ffn_d_codes, L0_ffn_d_absmax, L0_ffn_d_lora_a,
           L0_ffn_d_lora_b):
    x = np.asarray(L0_attn_hidden_states, dtype=np.float32)
    B = x.shape[0]
    common = {
        "qkv_w": np.ascontiguousarray(np.asarray(L0_attn_qkv_w, dtype=np.float32)),
        "qkv_b": np.ascontiguousarray(np.asarray(L0_attn_qkv_b, dtype=np.float32)),
        "out_w": np.ascontiguousarray(np.asarray(L0_attn_out_w, dtype=np.float32)),
        "out_b": np.ascontiguousarray(np.asarray(L0_attn_out_b, dtype=np.float32)),
        "g_codes": np.ascontiguousarray(np.asarray(L0_ffn_g_codes, dtype=np.int32)),
        "g_absmax": np.ascontiguousarray(np.asarray(L0_ffn_g_absmax, dtype=np.float32)),
        "u_codes": np.ascontiguousarray(np.asarray(L0_ffn_u_codes, dtype=np.int32)),
        "u_absmax": np.ascontiguousarray(np.asarray(L0_ffn_u_absmax, dtype=np.float32)),
        "d_codes": np.ascontiguousarray(np.asarray(L0_ffn_d_codes, dtype=np.int32)),
        "d_absmax": np.ascontiguousarray(np.asarray(L0_ffn_d_absmax, dtype=np.float32)),
        "g_la": np.ascontiguousarray(np.asarray(L0_ffn_g_lora_a, dtype=np.float32)),
        "g_lb": np.ascontiguousarray(np.asarray(L0_ffn_g_lora_b, dtype=np.float32)),
        "u_la": np.ascontiguousarray(np.asarray(L0_ffn_u_lora_a, dtype=np.float32)),
        "u_lb": np.ascontiguousarray(np.asarray(L0_ffn_u_lora_b, dtype=np.float32)),
        "d_la": np.ascontiguousarray(np.asarray(L0_ffn_d_lora_a, dtype=np.float32)),
        "d_lb": np.ascontiguousarray(np.asarray(L0_ffn_d_lora_b, dtype=np.float32)),
        "ones_row": np.ones((1, 512), dtype=BF16NP),
        "jcol": (np.arange(128, dtype=np.float32) % 64)[:, None],
    }
    in_maps = []
    for core in range(8):
        b, half = core // 2, core % 2
        own = np.arange(half * OWN, (half + 1) * OWN)
        other = np.arange((1 - half) * OWN, (2 - half) * OWN)
        order = np.concatenate([own, other])
        m = dict(common)
        m["hs"] = np.ascontiguousarray(x[b][order])
        m["kpos"] = np.ascontiguousarray(
            np.broadcast_to(order.astype(np.float32)[None, :], (128, S)))
        m["cmask"] = np.full((1, 128), MASKV if half == 0 else 0.0, dtype=BF16NP)
        in_maps.append(m)

    nc = build_program()
    res = run_bass_kernel_spmd(nc, in_maps, core_ids=list(range(8)))
    outp = np.empty((B, S, DM), dtype=np.float32)
    for core in range(8):
        b, half = core // 2, core % 2
        outp[b, half * OWN:(half + 1) * OWN, :] = res.results[core]["out"]
    return outp


# version 07 (final)
# speedup vs baseline: 1.0652x; 1.0629x over previous
import numpy as np
import ml_dtypes
from contextlib import ExitStack

import concourse.bass as bass
import concourse.mybir as mybir
from concourse.bass_utils import run_bass_kernel_spmd
from concourse.alu_op_type import AluOpType

F32 = mybir.dt.float32
BF16 = mybir.dt.bfloat16
I32 = mybir.dt.int32
AF = mybir.ActivationFunctionType
AX = mybir.AxisListType
BF16NP = ml_dtypes.bfloat16

DM, DMC = 4096, 32
S, OWN, NT, NTS = 2048, 1024, 8, 16
NH = 32
DFF, NS = 11008, 86
R = 16
EPS = 1e-5
MASKV = -66504.0
SCALE = 1.0 / float(np.sqrt(np.float64(128.0)))

NF4 = np.array([-1.0, -0.6961928009986877, -0.5250730514526367,
                -0.39491748809814453, -0.28444138169288635,
                -0.18477343022823334, -0.09105003625154495, 0.0,
                0.07958029955625534, 0.16093020141124725, 0.2461123019456863,
                0.33791524171829224, 0.44070982933044434, 0.5626170039176941,
                0.7229568362236023, 1.0], dtype=np.float32)
DELT = np.diff(NF4.astype(np.float64)).astype(np.float32)
GP_FIRST = 7
GU_F = 2048
D_CHUNKS = [1408] * 7 + [1152]
D_COFF = [sum(D_CHUNKS[:i]) for i in range(8)]

ENGS = ("sync", "tensor", "vector", "scalar", "gpsimd")
import os as _os
DEBUG_TAPS = bool(_os.environ.get("KDBG"))


def build_program():
    nc = bass.Bass("TRN2", target_bir_lowering=False, debug=False)

    def dram(name, shape, dt, kind=None):
        if kind:
            return nc.dram_tensor(name, shape, dt, kind=kind).ap()
        return nc.dram_tensor(name, shape, dt).ap()

    hs = dram("hs", [S, DM], F32, "ExternalInput")
    qkv_w = dram("qkv_w", [DM, 3 * DM], F32, "ExternalInput")
    qkv_b = dram("qkv_b", [3 * DM], F32, "ExternalInput")
    out_w = dram("out_w", [DM, DM], F32, "ExternalInput")
    out_b = dram("out_b", [DM], F32, "ExternalInput")
    g_codes = dram("g_codes", [DFF, DM], I32, "ExternalInput")
    g_absmax = dram("g_absmax", [DFF, DM // 64], F32, "ExternalInput")
    u_codes = dram("u_codes", [DFF, DM], I32, "ExternalInput")
    u_absmax = dram("u_absmax", [DFF, DM // 64], F32, "ExternalInput")
    d_codes = dram("d_codes", [DM, DFF], I32, "ExternalInput")
    d_absmax = dram("d_absmax", [DM, DFF // 64], F32, "ExternalInput")
    g_la = dram("g_la", [DM, R], F32, "ExternalInput")
    g_lb = dram("g_lb", [R, DFF], F32, "ExternalInput")
    u_la = dram("u_la", [DM, R], F32, "ExternalInput")
    u_lb = dram("u_lb", [R, DFF], F32, "ExternalInput")
    d_la = dram("d_la", [DFF, R], F32, "ExternalInput")
    d_lb = dram("d_lb", [R, DM], F32, "ExternalInput")
    kpos = dram("kpos", [128, S], F32, "ExternalInput")
    jcol = dram("jcol", [128, 1], F32, "ExternalInput")
    cmask = dram("cmask", [1, 128], BF16, "ExternalInput")
    ones_row = dram("ones_row", [1, 512], BF16, "ExternalInput")
    out = dram("out", [OWN, DM], F32, "ExternalOutput")

    xT_sc = dram("xT_sc", [NTS, DMC, 128, 128], BF16)
    qkvT_sc = dram("qkvT_sc", [96, NTS, 128, 128], BF16)
    wgT_sc = dram("wgT_sc", [NS, DMC, 128, 128], BF16)
    wuT_sc = dram("wuT_sc", [NS, DMC, 128, 128], BF16)
    wdT_sc = dram("wdT_sc", [NS, DMC, 128, 128], BF16)
    x3_sc = dram("x3_sc", [NT, NS, 128, 128], BF16)
    qkvw_bf = dram("qkvw_bf", [96, DMC, 128, 128], BF16)
    attn_sc = dram("attn_sc", [OWN, DM], F32)
    x1_sc = dram("x1_sc", [OWN, DM], F32)
    res2_sc = dram("res2_sc", [OWN, DM], F32)

    es = ExitStack()

    def sbuf(name, shape, dt):
        return es.enter_context(nc.sbuf_tensor(name, shape, dt))

    identb = sbuf("identb", [128, 128], BF16)
    identf = sbuf("identf", [128, 128], F32)
    triag = sbuf("triag", [128, 128], F32)
    invf = sbuf("invf", [128, 1], F32)
    biasq = sbuf("biasq", [128, 96], F32)
    biaso = sbuf("biaso", [128, 32], F32)
    onesb = sbuf("onesb", [1, 512], BF16)
    cmb = sbuf("cmb", [1, 128], BF16)
    amt = sbuf("amt", [128, 64], F32)
    lora_a = sbuf("lora_a", [128, 16 * 150], BF16)
    lbs = [sbuf(f"lbs{i}", [16, 128], BF16) for i in range(2)]
    lbf = [sbuf(f"lbf{i}", [16, 128], F32) for i in range(2)]
    tmp_l = sbuf("tmp_l", [16, 2 * OWN], BF16)
    tmp_d = sbuf("tmp_d", [16, OWN], F32)
    tmp_db = sbuf("tmp_db", [16, OWN], BF16)
    dlb_t = sbuf("dlb_t", [16, 512], BF16)
    dlb_f = sbuf("dlb_f", [16, 512], F32)
    stat = sbuf("stat", [128, 16], F32)
    st_sum = sbuf("st_sum", [128, NT], F32)
    st_sq = sbuf("st_sq", [128, NT], F32)

    A_bf = sbuf("A_bf", [128, 49152], BF16)
    A_f1 = sbuf("A_f1", [128, 12288], F32)
    A_f2 = sbuf("A_f2", [128, 8192], F32)
    A_bf_i = A_bf.ap().bitcast(I32)
    A_f2_b = A_f2.ap().bitcast(BF16)
    lora_as = A_f1[:, 4096:4096 + 16 * NS]
    swp = A_f2_b[:, 0:S + OWN]

    ps_all = es.enter_context(nc.psum_tensor("ps_all", [128, 4096], F32))

    def bank(b, w=1):
        return ps_all[:, b * 512:(b + w) * 512]

    streams = {e: [] for e in ENGS}
    sems = {}

    class _S:
        __slots__ = ("h", "n")

        def __init__(self, h):
            self.h = h
            self.n = 0

    def sem(name):
        if name not in sems:
            sems[name] = _S(es.enter_context(nc.semaphore(name)))
        return sems[name]

    def op(eng, emit, inc=None, k=1, wait=None):
        waits = []
        for w in (wait or []):
            nm, val = w
            if val > 0:
                waits.append((sem(nm).h, val))
        inch = None
        val_out = None
        if inc is not None:
            s = sem(inc)
            s.n += k
            val_out = s.n
            inch = s.h

        def thunk(E, emit=emit, waits=waits, inch=inch, k=k):
            for hh, vv in waits:
                E.wait_ge(hh, vv)
            inst = emit(E)
            if inch is not None:
                inst.then_inc(inch, k)
        streams[eng].append(thunk)
        return val_out

    def dma(outap, inap, inc="ld", wait=None, eng="sync"):
        return op(eng, lambda E: E.dma_start(out=outap, in_=inap), inc=inc,
                  k=16, wait=wait)

    def cur(name):
        return sem(name).n

    trc = [0]

    def transpose_block(src_ap, dst_ap, ident, wait_t=None, wait_s=None):
        i = trc[0]
        b = 4 + (i % 4)
        if src_ap.dtype == BF16:
            pv = bank(b).bitcast(BF16)[:, 0:128]
        else:
            pv = bank(b)[:, 0:128]
        wt = list(wait_t or [])
        if i >= 4:
            wt.append(("we", i - 3))
        op("tensor", lambda E, s=src_ap, idn=ident, pv=pv: E.transpose(pv, s, idn),
           inc="wt", wait=wt)
        ws = list(wait_s or [])
        ws.append(("wt", i + 1))
        ev = op("scalar", lambda E, d=dst_ap, pv=pv: E.copy(d, pv),
                inc="we", wait=ws)
        trc[0] += 1
        return ev

    def g_consts(E):
        E.memset(identb[:], 0.0)
        E.affine_select(out=identb[:], in_=identb[:], compare_op=AluOpType.not_equal,
                        fill=1.0, base=0, pattern=[[-1, 128]], channel_multiplier=1)
        E.memset(identf[:], 0.0)
        E.affine_select(out=identf[:], in_=identf[:], compare_op=AluOpType.not_equal,
                        fill=1.0, base=0, pattern=[[-1, 128]], channel_multiplier=1)
        E.memset(triag[:], 0.0)
        E.affine_select(out=triag[:], in_=triag[:], compare_op=AluOpType.is_ge,
                        fill=MASKV, base=0, pattern=[[-1, 128]], channel_multiplier=1)
        E.memset(tmp_d[:], 0.0)
        return E.engine_nop()
    op("gpsimd", g_consts, inc="gg")

    jld = dma(invf[:, 0:1], jcol[:, :])
    op("scalar", lambda E: E.activation(out=invf[:, 0:1], in_=invf[:, 0:1],
                                        func=AF.Exp,
                                        scale=-float(np.log(10000.0) / 64.0)),
       inc="ev", wait=[("ld", jld)])

    dma(biasq[:, :], qkv_b.rearrange("(c p) -> p c", p=128))
    dma(biaso[:, :], out_b.rearrange("(c p) -> p c", p=128))
    dma(onesb[:, :], ones_row[:, :])
    dma(cmb[:, :], cmask[:, :])
    dma(lora_as[:, 0:16 * DMC].rearrange("p (c r) -> p c r", r=16),
        g_la.rearrange("(c p) r -> p c r", p=128))
    dma(lora_as[:, 16 * DMC:32 * DMC].rearrange("p (c r) -> p c r", r=16),
        u_la.rearrange("(c p) r -> p c r", p=128))
    op("vector", lambda E: E.tensor_copy(lora_a[:, 0:16 * 64], lora_as[:, 0:16 * 64]),
       inc="dv", wait=[("ld", cur("ld"))])
    dma(lora_as[:, 0:16 * NS].rearrange("p (c r) -> p c r", r=16),
        d_la.rearrange("(c p) r -> p c r", p=128), wait=[("dv", cur("dv"))])
    op("vector", lambda E: E.tensor_copy(lora_a[:, 16 * 64:16 * 150],
                                         lora_as[:, 0:16 * NS]),
       inc="dv", wait=[("ld", cur("ld"))])

    ktab_c = A_f1[:, 0:S]
    ktab_s = A_f1[:, S:2 * S]
    angt = A_f1[:, 2 * S:3 * S]
    rop1 = A_f1[:, 3 * S:4 * S]
    rop2 = A_f1[:, 4 * S:5 * S]
    kposb = A_f1[:, 5 * S:6 * S]

    kldv = dma(kposb, kpos[:, :])

    def v_ang(E):
        E.tensor_mul(angt, kposb, invf.ap()[:, 0:1].broadcast_to((128, S)))
        inv2pi = float(1.0 / (2 * np.pi))
        E.tensor_scalar(out=rop2, in0=angt, scalar1=inv2pi, scalar2=0.5,
                        op0=AluOpType.mult, op1=AluOpType.add)
        E.tensor_copy(rop1.bitcast(I32), rop2)
        E.tensor_copy(rop1, rop1.bitcast(I32))
        C1 = float(np.float32(2 * np.pi))
        C2 = float(np.float64(2 * np.pi) - np.float64(np.float32(2 * np.pi)))
        E.scalar_tensor_tensor(out=angt, in0=rop1, scalar=-C1, in1=angt,
                               op0=AluOpType.mult, op1=AluOpType.add)
        E.scalar_tensor_tensor(out=angt, in0=rop1, scalar=-C2, in1=angt,
                               op0=AluOpType.mult, op1=AluOpType.add)
        E.tensor_scalar(out=rop2, in0=angt, scalar1=float(np.pi),
                        scalar2=float(-2 * np.pi), op0=AluOpType.is_gt,
                        op1=AluOpType.mult)
        E.tensor_add(angt, angt, rop2)
        E.tensor_scalar(out=rop2, in0=angt, scalar1=float(-np.pi),
                        scalar2=float(2 * np.pi), op0=AluOpType.is_lt,
                        op1=AluOpType.mult)
        return E.tensor_add(angt, angt, rop2)
    op("vector", v_ang, inc="dv", wait=[("ld", kldv), ("ev", cur("ev"))])
    op("scalar", lambda E: E.activation(out=ktab_s, in_=angt, func=AF.Sin),
       inc="ev", wait=[("dv", cur("dv"))])

    def v_cosarg(E):
        E.tensor_scalar(out=rop1, in0=angt, scalar1=float(np.pi / 2), scalar2=None,
                        op0=AluOpType.add)
        E.tensor_scalar(out=rop2, in0=rop1, scalar1=float(np.pi),
                        scalar2=float(-2 * np.pi), op0=AluOpType.is_gt,
                        op1=AluOpType.mult)
        return E.tensor_add(rop1, rop1, rop2)
    op("vector", v_cosarg, inc="dv", wait=[("ev", cur("ev"))])
    op("scalar", lambda E: E.activation(out=ktab_c, in_=rop1, func=AF.Sin),
       inc="ev", wait=[("dv", cur("dv"))])

    xin = [A_f2[:, 0:4096], A_f2[:, 4096:8192]]
    xstg = [A_bf[:, 0:4096], A_bf[:, 4096:8192]]
    p1_st = {}
    p1_tr_end = {}
    for tt in range(NTS):
        w = []
        if tt >= 2:
            w.append(("we", p1_tr_end[tt - 2]))
            w.append(("st", p1_st[tt - 2]))
        ldv = dma(xin[tt % 2], hs[tt * 128:(tt + 1) * 128, :], wait=w)
        ev = None
        for k in range(DMC):
            ev = transpose_block(
                xin[tt % 2][:, k * 128:(k + 1) * 128],
                xstg[tt % 2].rearrange("p (k c) -> p k c", c=128)[:, k, :],
                identf[:],
                wait_t=[("ld", ldv)] if k == 0 else None)
        p1_tr_end[tt] = ev
        p1_st[tt] = dma(xT_sc[tt].rearrange("k p c -> p k c"),
                        xstg[tt % 2].rearrange("p (k c) -> p k c", c=128),
                        inc="st", wait=[("we", ev)])
    ST_P1 = cur("st")

    xTh = A_bf[:, 8192:8192 + DMC * 1024]
    wcol = [A_f2[:, 0:4096], A_f2[:, 4096:8192]]
    wbf = [A_bf[:, 0:4096], A_bf[:, 4096:8192]]
    qstg = [A_bf[:, 40960:40960 + 1024], A_bf[:, 41984:41984 + 1024]]
    ofs_half = [list(range(96)), list(range(32, 96))]
    p2_cast, p2_mm, p2_ev, p2_st = {}, {}, {}, {}
    for half in range(2):
        ofl = ofs_half[half]
        xw = [("st", ST_P1)]
        if half == 1:
            xw.append(("pe", p2_mm[(0, 95)]))
        xldv = None
        for t8 in range(8):
            xldv = dma(
                xTh.rearrange("p (k t c) -> p t k c", k=DMC, c=128)[:, t8],
                xT_sc[half * 8 + t8].rearrange("k p c -> p k c"),
                wait=xw if t8 == 0 else None)
        for j, of in enumerate(ofl):
            key = (half, of)
            wld = []
            if j >= 2:
                wld.append(("pe", p2_mm[(half, ofl[j - 2])]))
            if half == 0:
                wl = dma(wcol[j % 2].rearrange("p (k c) -> p k c", c=128),
                         qkv_w.rearrange("(k p) (o c) -> o k p c", p=128, c=128)[of].rearrange("k p c -> p k c"),
                         wait=wld)
                p2_cast[key] = op(
                    "vector", lambda E, j=j: E.tensor_copy(wbf[j % 2], wcol[j % 2]),
                    inc="dv", wait=[("ld", wl)] + wld)
                dma(qkvw_bf[of].rearrange("k p c -> p k c"),
                    wbf[j % 2].rearrange("p (k c) -> p k c", c=128),
                    inc="st", wait=[("dv", p2_cast[key])])
                wsrc = ("dv", p2_cast[key])
            else:
                wl = dma(wbf[j % 2].rearrange("p (k c) -> p k c", c=128),
                         qkvw_bf[of].rearrange("k p c -> p k c"), wait=wld)
                wsrc = ("ld", wl)

            def t_mm(E, j=j):
                pb = (j % 2) * 2
                last = None
                for k in range(DMC):
                    lhs = wbf[j % 2][:, k * 128:(k + 1) * 128]
                    for tg in range(2):
                        last = E.matmul(bank(pb + tg), lhs,
                                        xTh[:, k * 1024 + tg * 512:k * 1024 + (tg + 1) * 512],
                                        start=(k == 0), stop=(k == DMC - 1))
                return last
            wmm = [("ld", xldv), wsrc]
            if j >= 2:
                wmm.append(("ev", p2_ev[(half, ofl[j - 2])]))
            p2_mm[key] = op("tensor", t_mm, inc="pe", wait=wmm)

            def s_ev_f(E, j=j, of=of):
                pb = (j % 2) * 2
                last = None
                for tg in range(2):
                    last = E.tensor_add(
                        qstg[j % 2].rearrange("p (t c) -> p t c", c=512)[:, tg, :],
                        bank(pb + tg),
                        biasq.ap()[:, of:of + 1].broadcast_to((128, 512)))
                return last
            wev = [("pe", p2_mm[key])]
            if j >= 2:
                wev.append(("st", p2_st[(half, ofl[j - 2])]))
            p2_ev[key] = op("vector", s_ev_f, inc="ev", wait=wev)
            p2_st[key] = dma(qkvT_sc[of, half * 8:half * 8 + 8].rearrange("t p c -> p t c"),
                             qstg[j % 2].rearrange("p (t c) -> p t c", c=128),
                             inc="st", wait=[("ev", p2_ev[key])])
    ST_P2 = cur("st")

    kb = [A_bf[:, 0:2048], A_bf[:, 2048:4096]]
    qb = [A_bf[:, 4096:5120], A_bf[:, 5120:6144]]
    vbt = [A_bf[:, 6144:8192], A_bf[:, 8192:10240]]
    pbuf = A_bf[:, 10240:12288]
    ptb = A_bf[:, 12288:14336]
    ctx_big = A_bf[:, 14336:14336 + NH * 1024]
    rpA = A_f1[:, 2 * S:3 * S]
    rpB = A_f1[:, 3 * S:4 * S]

    p3_rope, p3_cev = {}, {}
    dbg_hold = []
    pv_last = 0
    if DEBUG_TAPS:
        d_krope = dram("dbg_krope", [128, S], BF16, "ExternalOutput")
        d_qrope = dram("dbg_qrope", [128, OWN], BF16, "ExternalOutput")
        d_p00 = dram("dbg_p00", [128, S], BF16, "ExternalOutput")
        d_vt0 = dram("dbg_vt0", [128, S], BF16, "ExternalOutput")
        d_ctx0 = dram("dbg_ctx0", [128, OWN], BF16, "ExternalOutput")
        d_s00 = dram("dbg_s00", [128, S], F32, "ExternalOutput")
    for h in range(NH):
        e = h % 2
        w = [("st", ST_P2)]
        if h >= 2:
            w.append(("pe", pv_last))
            w.append(("ev", p3_cev[h - 2]))
        dma(kb[e].rearrange("p (t c) -> p t c", c=128),
            qkvT_sc[32 + h].rearrange("t p c -> p t c"), wait=w)
        dma(qb[e].rearrange("p (t c) -> p t c", c=128),
            qkvT_sc[h, 0:8].rearrange("t p c -> p t c"))
        ldv = dma(vbt[e].rearrange("p (t c) -> p t c", c=128),
                  qkvT_sc[64 + h].rearrange("t p c -> p t c"))
        sw = [("ld", ldv)]
        if h >= 1:
            sw.append(("dv", p3_rope[h - 1]))
        dma(swp[0:64, 0:S], kb[e][64:128, 0:S], wait=sw)
        dma(swp[64:128, 0:S], kb[e][0:64, 0:S])
        dma(swp[0:64, S:S + OWN], qb[e][64:128, 0:OWN])
        swv = dma(swp[64:128, S:S + OWN], qb[e][0:64, 0:OWN])

        def v_rope(E, e=e):
            E.tensor_mul(rpA[0:64, 0:S], kb[e][0:64, 0:S], ktab_c[0:64, 0:S])
            E.tensor_mul(rpB[0:64, 0:S], swp[0:64, 0:S], ktab_s[0:64, 0:S])
            E.tensor_sub(kb[e][0:64, 0:S], rpA[0:64, 0:S], rpB[0:64, 0:S])
            E.tensor_mul(rpA[64:128, 0:S], kb[e][64:128, 0:S], ktab_c[64:128, 0:S])
            E.tensor_mul(rpB[64:128, 0:S], swp[64:128, 0:S], ktab_s[64:128, 0:S])
            E.tensor_add(kb[e][64:128, 0:S], rpA[64:128, 0:S], rpB[64:128, 0:S])
            E.tensor_mul(rpA[0:64, 0:OWN], qb[e][0:64, 0:OWN], ktab_c[0:64, 0:OWN])
            E.tensor_mul(rpB[0:64, 0:OWN], swp[0:64, S:S + OWN], ktab_s[0:64, 0:OWN])
            E.tensor_sub(qb[e][0:64, 0:OWN], rpA[0:64, 0:OWN], rpB[0:64, 0:OWN])
            E.tensor_mul(rpA[64:128, 0:OWN], qb[e][64:128, 0:OWN], ktab_c[64:128, 0:OWN])
            E.tensor_mul(rpB[64:128, 0:OWN], swp[64:128, S:S + OWN], ktab_s[64:128, 0:OWN])
            return E.tensor_add(qb[e][64:128, 0:OWN], rpA[64:128, 0:OWN],
                                rpB[64:128, 0:OWN])
        p3_rope[h] = op("vector", v_rope, inc="dv", wait=[("ld", swv)])
        if DEBUG_TAPS and h == 0:
            dma(d_krope, kb[e][:, 0:S], inc="st", wait=[("dv", p3_rope[h])])
            dma(d_qrope, qb[e][:, 0:OWN], inc="st", wait=[("dv", p3_rope[h])])

        vev = None
        for j in range(NTS):
            vev = transpose_block(
                vbt[e][:, j * 128:(j + 1) * 128],
                vbt[e].rearrange("p (t c) -> p t c", c=128)[:, j, :],
                identb[:],
                wait_t=[("dv", p3_rope[h])] if j == 0 else None)

        if DEBUG_TAPS and h == 0:
            dma(d_vt0, vbt[e][:, 0:S], inc="st", wait=[("we", vev)])
        for qt in range(NT):
            eo = (qt + 1) * 128

            def t_scores(E, e=e, qt=qt, eo=eo):
                lhs = qb[e][:, qt * 128:(qt + 1) * 128]
                for gi in range((eo + 511) // 512):
                    c0 = gi * 512
                    c1 = min(eo, c0 + 512)
                    E.matmul(ps_all[:, c0:c1], lhs, kb[e][:, c0:c1],
                             start=True, stop=True)
                last = None
                for gi in range(2):
                    c0 = 1024 + gi * 512
                    E.matmul(ps_all[:, c0:c0 + 512], lhs, kb[e][:, c0:c0 + 512],
                             start=True, stop=False)
                    last = E.matmul(ps_all[:, c0:c0 + 512], cmb[:, :], onesb[:, :],
                                    start=False, stop=True)
                return last
            wsc = []
            if qt == 0:
                wsc += [("dv", p3_rope[h]), ("we", vev)]
            else:
                wsc.append(("ev", p3_cev[(h, qt - 1)]))
            wsc += dbg_hold
            dbg_hold.clear() if dbg_hold else None
            mmv = op("tensor", t_scores, inc="pe", wait=wsc)

            mskv = op("vector",
                      lambda E, qt=qt: E.tensor_add(
                          ps_all[:, qt * 128:(qt + 1) * 128],
                          ps_all[:, qt * 128:(qt + 1) * 128], triag[:]),
                      inc="dv", wait=[("pe", mmv)])

            def s_exp(E, eo=eo):
                E.activation(out=rpA[:, 0:eo], in_=ps_all[:, 0:eo], func=AF.Exp,
                             scale=SCALE, accum_out=stat[:, 0:1])
                return E.activation(out=rpA[:, 1024:2048], in_=ps_all[:, 1024:2048],
                                    func=AF.Exp, scale=SCALE, accum_out=stat[:, 1:2])
            expv = op("scalar", s_exp, inc="ev", wait=[("dv", mskv)])

            sumv = op("vector",
                      lambda E: E.tensor_add(stat[:, 2:3], stat[:, 0:1],
                                             stat[:, 1:2]),
                      inc="dv", wait=[("ev", expv)])

            def s_recip(E):
                E.activation(out=stat[:, 4:5], in_=stat[:, 2:3], func=AF.Ln)
                return E.activation(out=stat[:, 3:4], in_=stat[:, 4:5],
                                    func=AF.Exp, scale=-1.0)
            rcpv = op("scalar", s_recip, inc="ev", wait=[("dv", sumv)])

            def v_norm(E, eo=eo):
                E.tensor_mul(pbuf[:, 0:eo], rpA[:, 0:eo],
                             stat.ap()[:, 3:4].broadcast_to((128, eo)))
                return E.tensor_mul(pbuf[:, 1024:2048], rpA[:, 1024:2048],
                                    stat.ap()[:, 3:4].broadcast_to((128, 1024)))
            nrmv = op("vector", v_norm, inc="dv", wait=[("ev", rcpv)])
            if DEBUG_TAPS and h == 0 and qt == 0:
                svev = op("scalar", lambda E: E.copy(rop2, ps_all[:, 0:S]),
                          inc="ev", wait=[("dv", mskv)])
                dbg_hold.append(("st", dma(d_s00, rop2, inc="st",
                                           wait=[("ev", svev)])))
                dbg_hold.append(("st", dma(d_p00, pbuf[:, 0:S], inc="st",
                                           wait=[("dv", nrmv)])))

            cols = [j * 128 for j in range(eo // 128)] + \
                   [1024 + j * 128 for j in range(8)]
            lastev = None
            for j, col in enumerate(cols):
                lastev = transpose_block(
                    pbuf[:, col:col + 128],
                    ptb.rearrange("p (t c) -> p t c", c=128)[:, j, :],
                    identb[:],
                    wait_t=[("dv", nrmv)] if j == 0 else None)

            def t_pv(E, e=e, cols=cols):
                last = None
                for j, col in enumerate(cols):
                    last = E.matmul(bank(0)[:, 0:128], vbt[e][:, col:col + 128],
                                    ptb[0:128, j * 128:(j + 1) * 128],
                                    start=(j == 0), stop=(j == len(cols) - 1))
                return last
            pv_last = op("tensor", t_pv, inc="pe", wait=[("we", lastev)])
            p3_cev[(h, qt)] = op(
                "scalar",
                lambda E, h=h, qt=qt: E.copy(
                    ctx_big[:, h * 1024 + qt * 128:h * 1024 + (qt + 1) * 128],
                    bank(0)[:, 0:128]),
                inc="ev", wait=[("pe", pv_last)])
        if DEBUG_TAPS and h == 0:
            dma(d_ctx0, ctx_big[:, 0:OWN], inc="st",
                wait=[("ev", p3_cev[(0, NT - 1)])])
        p3_cev[h] = p3_cev[(h, NT - 1)]
    P3_EV = cur("ev")

    wcol4 = [A_f2[:, 0:4096], A_f2[:, 4096:8192]]
    wbf4 = [A_bf[:, 0:4096], A_bf[:, 4096:8192]]
    attn_ev = A_f1[:, 0:1024]
    tstg4 = A_f1[:, 1024:1152]
    p4_mm, p4_ev = {}, {}
    p4_blk_st = {}
    for of in range(DMC):
        wld = []
        if of >= 2:
            wld.append(("pe", p4_mm[of - 2]))
        else:
            wld.append(("pe", pv_last))
            wld.append(("ev", P3_EV))
        wl = dma(wcol4[of % 2].rearrange("p (k c) -> p k c", c=128),
                 out_w.rearrange("(k p) (o c) -> o k p c", p=128, c=128)[of].rearrange("k p c -> p k c"),
                 wait=wld)
        cv = op("vector", lambda E, of=of: E.tensor_copy(wbf4[of % 2], wcol4[of % 2]),
                inc="dv", wait=[("ld", wl)] + wld)

        def t_op(E, of=of):
            pb = (of % 2) * 2
            last = None
            for k in range(DMC):
                lhs = wbf4[of % 2][:, k * 128:(k + 1) * 128]
                for tg in range(2):
                    last = E.matmul(bank(pb + tg), lhs,
                                    ctx_big[:, k * 1024 + tg * 512:k * 1024 + (tg + 1) * 512],
                                    start=(k == 0), stop=(k == DMC - 1))
            return last
        wmm = [("dv", cv), ("ev", P3_EV)]
        if of >= 2:
            wmm.append(("ev", p4_ev[of - 2]))
        p4_mm[of] = op("tensor", t_op, inc="pe", wait=wmm)

        def s_ev4(E, of=of):
            pb = (of % 2) * 2
            last = None
            for tg in range(2):
                last = E.tensor_add(
                    attn_ev.rearrange("p (t c) -> p t c", c=512)[:, tg, :],
                    bank(pb + tg),
                    biaso.ap()[:, of:of + 1].broadcast_to((128, 512)))
            return last
        wev = [("pe", p4_mm[of])]
        if of >= 1:
            wev.append(("st", p4_blk_st[of - 1]))
        p4_ev[of] = op("vector", s_ev4, inc="ev", wait=wev)

        for j in range(NT):
            ev = transpose_block(
                attn_ev.rearrange("p (t c) -> p t c", c=128)[:, j, :],
                tstg4, identf[:],
                wait_t=[("ev", p4_ev[of])] if j == 0 else
                       [("st", cur("st"))])
            p4_blk_st[of] = dma(
                attn_sc.rearrange("(t p) (o c) -> o t p c", p=128, c=128)[of, j],
                tstg4, inc="st", wait=[("we", ev)])

    x1T = A_bf[:, 14336:14336 + DMC * 1024]
    x1bf = A_bf[:, 0:4096]
    ln_a = A_f2[:, 0:4096]
    ln_b = A_f2[:, 4096:8192]
    res_t = A_f1[:, 2048:6144]
    cent = A_f1[:, 6144:10240]
    P4_ST_ALL = cur("st")
    for tt in range(NT):
        la = dma(ln_a, attn_sc[tt * 128:(tt + 1) * 128, :],
                 wait=[("st", P4_ST_ALL)] if tt == 0 else [("dv", cur("dv"))])
        lb = dma(ln_b, hs[tt * 128:(tt + 1) * 128, :])

        dv0 = op("vector", lambda E: E.tensor_add(res_t, ln_a, ln_b),
                 inc="dv", wait=[("ld", lb)])
        sv0 = op("scalar", lambda E: E.activation(out=ln_a, in_=res_t, func=AF.Copy,
                                                  accum_out=stat[:, 0:1]),
                 inc="ev", wait=[("dv", dv0)])

        dvs = op("vector", lambda E: E.tensor_scalar(
            out=stat[:, 1:2], in0=stat[:, 0:1], scalar1=float(1.0 / DM),
            scalar2=None, op0=AluOpType.mult), inc="dv", wait=[("ev", sv0)])
        gdv = op("gpsimd", lambda E: E.tensor_copy(stat[:, 6:7], stat[:, 1:2]),
                 inc="gg", wait=[("dv", dvs)])
        dvv = op("vector", lambda E: E.tensor_sub(
            cent, res_t, stat.ap()[:, 6:7].broadcast_to((128, DM))),
            inc="dv", wait=[("gg", gdv)])

        sq = op("scalar", lambda E: E.activation(out=ln_a, in_=cent, func=AF.Square,
                                                 accum_out=stat[:, 2:3]),
                inc="ev", wait=[("dv", dvv)])

        def v_var(E):
            E.tensor_scalar(out=stat[:, 3:4], in0=stat[:, 2:3],
                            scalar1=float(1.0 / DM), scalar2=float(EPS),
                            op0=AluOpType.mult, op1=AluOpType.add)
            return E.engine_nop()
        vv = op("vector", v_var, inc="dv", wait=[("ev", sq)])
        def s_rsq(E):
            E.activation(out=stat[:, 4:5], in_=stat[:, 3:4], func=AF.Ln)
            return E.activation(out=stat[:, 5:6], in_=stat[:, 4:5], func=AF.Exp,
                                scale=-0.5)
        sqv = op("scalar", s_rsq, inc="ev", wait=[("dv", vv)])

        def v_x1(E):
            E.tensor_mul(ln_b, cent, stat.ap()[:, 5:6].broadcast_to((128, DM)))
            return E.tensor_copy(x1bf, ln_b)
        x1v = op("vector", v_x1, inc="dv",
                 wait=[("ev", sqv), ("we", cur("we"))])
        dma(x1_sc[tt * 128:(tt + 1) * 128, :], ln_b, inc="st", wait=[("dv", x1v)])
        for k in range(DMC):
            transpose_block(
                x1bf[:, k * 128:(k + 1) * 128],
                x1T.rearrange("p (k t c) -> p k t c", k=DMC, c=128)[:, k, tt, :],
                identb[:],
                wait_t=[("dv", x1v)] if k == 0 else None)
    LN1_WE = cur("we")

    deq_tiles = []
    for s in range(NS):
        for mat in ("g", "u"):
            for jh in range(2):
                deq_tiles.append((mat, s, jh))
    for rs in range(DMC):
        for cc in range(8):
            deq_tiles.append(("d", rs, cc))

    dq_codes = [A_bf_i[:, 0:2048], A_bf_i[:, 2048:4096]]
    dq_w = [A_bf[:, 8192:10240], A_bf[:, 10240:12288]]
    acc = [A_f1[:, 0:2048], A_f1[:, 2048:4096]]
    term = A_f1[:, 4096:6144]
    gterm = [A_f1[:, 6144:8192], A_f1[:, 8192:10240]]
    gacc = [A_f2[:, 0:2048], A_f2[:, 2048:4096]]

    deq_done = {}
    gu_ws = {}
    prev_wv = {}

    n_gp = 15 - GP_FIRST
    part2 = {}
    for ti, tile in enumerate(deq_tiles):
        kind = tile[0]
        if kind in ("g", "u"):
            _, s, jh = tile
            codes_src = (g_codes if kind == "g" else u_codes)[
                s * 128:(s + 1) * 128, jh * GU_F:(jh + 1) * GU_F]
            am_src = (g_absmax if kind == "g" else u_absmax)[
                s * 128:(s + 1) * 128, jh * (GU_F // 64):(jh + 1) * (GU_F // 64)]
            F = GU_F
            wt_dst = (wgT_sc if kind == "g" else wuT_sc)[s, jh * 16:jh * 16 + 16]
        else:
            _, rs, cc = tile
            F = D_CHUNKS[cc]
            codes_src = d_codes[rs * 128:(rs + 1) * 128, D_COFF[cc]:D_COFF[cc] + F]
            am_src = d_absmax[rs * 128:(rs + 1) * 128,
                              D_COFF[cc] // 64:(D_COFF[cc] + F) // 64]
            wt_dst = wdT_sc[D_COFF[cc] // 128:(D_COFF[cc] + F) // 128, rs]
        nb = F // 64
        e = ti % 2
        cw = []
        if ti >= 2:
            cw.append(("wv", prev_wv[ti - 2]))
        else:
            cw.append(("we", LN1_WE))
        dma(dq_codes[e][:, 0:F], codes_src, inc="cd", wait=cw)
        cdv = dma(amt[:, e * 32:e * 32 + nb], am_src, inc="cd")

        gq_base = cur("gq")

        def v_seed(E, e=e, F=F):
            return E.tensor_scalar(out=acc[e][:, 0:F], in0=dq_codes[e][:, 0:F],
                                   scalar1=1.0, scalar2=float(DELT[0]),
                                   op0=AluOpType.is_ge, op1=AluOpType.mult)
        op("vector", v_seed, inc="dv", wait=[("cd", cdv)])

        gp_targets = []
        for gi, di in enumerate(range(GP_FIRST, 15)):
            gb = gterm[gi % 2]
            wgt = []
            if gi >= 2:
                wgt.append(("gq", gq_base + gi - 1))
            elif ti >= 1:
                last_gi = n_gp - 1 - ((n_gp - 1 - (gi % 2)) % 2)
                wgt.append(("gq", gq_base - (n_gp - 1 - last_gi)))

            def v_gterm(E, e=e, F=F, di=di, gb=gb):
                return E.tensor_scalar(out=gb[:, 0:F], in0=dq_codes[e][:, 0:F],
                                       scalar1=float(di + 1), scalar2=float(DELT[di]),
                                       op0=AluOpType.is_ge, op1=AluOpType.mult)
            gp_targets.append(op("vector", v_gterm, inc="dq", wait=wgt))

            if 1 <= gi < GP_FIRST:
                def v_term(E, e=e, F=F, di=gi):
                    E.tensor_scalar(out=term[:, 0:F], in0=dq_codes[e][:, 0:F],
                                    scalar1=float(di + 1), scalar2=float(DELT[di]),
                                    op0=AluOpType.is_ge, op1=AluOpType.mult)
                    return E.tensor_add(acc[e][:, 0:F], acc[e][:, 0:F],
                                        term[:, 0:F])
                op("vector", v_term, inc="dv")

        for gi in range(n_gp):
            gb = gterm[gi % 2]
            if gi == 0:
                op("gpsimd", lambda E, gb=gb, F=F, e=e: E.tensor_copy(
                    gacc[e][:, 0:F], gb[:, 0:F]),
                   inc="gq", wait=[("dq", gp_targets[0])])
            else:
                op("gpsimd", lambda E, gb=gb, F=F, e=e: E.tensor_add(
                    gacc[e][:, 0:F], gacc[e][:, 0:F], gb[:, 0:F]),
                   inc="gq", wait=[("dq", gp_targets[gi])])
        op("gpsimd", lambda E: E.engine_nop(), inc="gp")
        gpv = cur("gp")

        def mk_part2(ti=ti, tile=tile, e=e, F=F, nb=nb, wt_dst=wt_dst,
                     gpv=gpv, kind=kind):
            def emit():
                def v_merge(E):
                    return E.tensor_add(acc[e][:, 0:F], acc[e][:, 0:F],
                                        gacc[e][:, 0:F])
                mw = [("gp", gpv)]
                if ti >= 2:
                    mw.append(("ws", deq_done[ti - 2]))
                op("vector", v_merge, inc="dv", wait=mw)

                def v_w(E):
                    am_b = amt.ap()[:, e * 32:e * 32 + nb].unsqueeze(2).broadcast_to((128, nb, 64))
                    return E.scalar_tensor_tensor(
                        out=dq_w[e][:, 0:F].rearrange("p (b k) -> p b k", b=nb),
                        in0=acc[e][:, 0:F].rearrange("p (b k) -> p b k", b=nb),
                        scalar=float(NF4[0]), in1=am_b,
                        op0=AluOpType.add, op1=AluOpType.mult)
                prev_wv[ti] = op("vector", v_w, inc="wv")

                ev = None
                for k in range(F // 128):
                    ev = transpose_block(
                        dq_w[e][:, k * 128:(k + 1) * 128],
                        dq_w[e].rearrange("p (k c) -> p k c", c=128)[:, k, :],
                        identb[:],
                        wait_t=[("wv", prev_wv[ti])] if k == 0 else None)
                deq_done[ti] = dma(wt_dst.rearrange("k p c -> p k c"),
                                   dq_w[e][:, 0:F].rearrange("p (k c) -> p k c", c=128),
                                   inc="ws", wait=[("we", ev)], eng="scalar")
                if kind in ("g", "u") and tile[2] == 1:
                    gu_ws[(kind, tile[1])] = deq_done[ti]
            return emit
        part2[ti] = mk_part2()
        if ti >= 1:
            part2[ti - 1]()
    part2[len(deq_tiles) - 1]()
    DEQ_WS = cur("ws")

    def t_la(E):
        last = None
        for m in range(2):
            for k in range(DMC):
                lhs = lora_a[:, (m * DMC + k) * 16:(m * DMC + k) * 16 + 16]
                for tg in range(2):
                    last = E.matmul(ps_all[0:16, 2048 + m * 1024 + tg * 512:
                                           2048 + m * 1024 + (tg + 1) * 512],
                                    lhs,
                                    x1T[:, k * 1024 + tg * 512:k * 1024 + (tg + 1) * 512],
                                    start=(k == 0), stop=(k == DMC - 1))
        return last
    lav = op("tensor", t_la, inc="pe",
             wait=[("we", cur("we")), ("dv", cur("dv")), ("ws", cur("ws"))])
    lev = op("scalar", lambda E: E.copy(tmp_l[:, :], ps_all[0:16, 2048:4096]),
             inc="ev", wait=[("pe", lav)])

    W3 = [A_bf[:, 0:4096], A_bf[:, 4096:8192], A_bf[:, 8192:12288]]
    w3_user = [0, 0, 0]
    r1 = A_f2_b[:, 0:1024]
    x3t = A_f2_b[:, 1024:2048]
    p5_mm = {}
    p5_x3 = {}
    for s in range(NS):
        bg = (2 * s) % 3
        bu = (2 * s + 1) % 3
        wldg = dma(W3[bg].rearrange("p (k c) -> p k c", c=128),
                   wgT_sc[s].rearrange("k p c -> p k c"),
                   wait=[("ws", gu_ws[("g", s)] if s > 0 else cur("ws"))] +
                        ([("pe", w3_user[bg])] if w3_user[bg] else []))
        wldu = dma(W3[bu].rearrange("p (k c) -> p k c", c=128),
                   wuT_sc[s].rearrange("k p c -> p k c"),
                   wait=[("ws", gu_ws[("u", s)])] +
                        ([("pe", w3_user[bu])] if w3_user[bu] else []))
        lbl = dma(lbf[s % 2][:, :], g_lb[:, s * 128:(s + 1) * 128],
                  wait=[("dv", p5_x3[s - 2])] if s >= 2 else [])
        lbl2 = dma(dlb_f[:, 0:128], u_lb[:, s * 128:(s + 1) * 128])
        lbc = op("vector", lambda E, s=s: (E.tensor_copy(lbs[s % 2][:, :], lbf[s % 2][:, :]),
                                           E.tensor_copy(dlb_t[:, 0:128], dlb_f[:, 0:128]))[-1],
                 inc="dv", wait=[("ld", lbl2)])

        def t_gu(E, s=s, bg=bg, bu=bu):
            last = None
            for m, bb in ((0, bg), (1, bu)):
                pb = m * 2
                for k in range(DMC):
                    lhs = W3[bb][:, k * 128:(k + 1) * 128]
                    for tg in range(2):
                        last = E.matmul(bank(pb + tg), lhs,
                                        x1T[:, k * 1024 + tg * 512:k * 1024 + (tg + 1) * 512],
                                        start=(k == 0), stop=False)
                lb = lbs[s % 2] if m == 0 else dlb_t[:, 0:128]
                for tg in range(2):
                    last = E.matmul(bank(pb + tg), lb[0:16, :],
                                    tmp_l[:, m * 1024 + tg * 512:m * 1024 + (tg + 1) * 512],
                                    start=False, stop=True)
            return last
        wmm = [("ld", wldu), ("dv", lbc), ("ev", lev)]
        if s >= 1:
            wmm.append(("dv", p5_x3[s - 1]))
        p5_mm[s] = op("tensor", t_gu, inc="pe", wait=wmm)
        w3_user[bg] = p5_mm[s]
        w3_user[bu] = p5_mm[s]

        rv = op("scalar", lambda E: (E.activation(out=r1.rearrange("p (t c) -> p t c", c=512)[:, 0, :],
                                                  in_=bank(0), func=AF.Relu),
                                     E.activation(out=r1.rearrange("p (t c) -> p t c", c=512)[:, 1, :],
                                                  in_=bank(1), func=AF.Relu))[-1],
                inc="ev", wait=[("pe", p5_mm[s])])

        def v_x3(E):
            E.tensor_mul(x3t[:, 0:512], r1[:, 0:512], bank(2).bitcast(F32) if False else bank(2))
            return E.tensor_mul(x3t[:, 512:1024], r1[:, 512:1024], bank(3))
        x3w = [("ev", rv)]
        if s >= 1:
            x3w.append(("st", cur("st")))
        p5_x3[s] = op("vector", v_x3, inc="dv", wait=x3w)

        def t_dl(E, s=s):
            lhs = lora_a[:, (64 + s) * 16:(64 + s) * 16 + 16]
            E.matmul(ps_all[0:16, 3072:3584], lhs, x3t[:, 0:512],
                     start=True, stop=True)
            return E.matmul(ps_all[0:16, 3584:4096], lhs, x3t[:, 512:1024],
                            start=True, stop=True)
        dlv = op("tensor", t_dl, inc="pe", wait=[("dv", p5_x3[s])])
        op("vector", lambda E: E.scalar_tensor_tensor(
            out=tmp_d[:, :], in0=ps_all[0:16, 3072:4096], scalar=1.0,
            in1=tmp_d[:, :], op0=AluOpType.mult, op1=AluOpType.add),
           inc="dv", wait=[("pe", dlv)])
        dma(x3_sc[:, s].rearrange("t p c -> p t c"),
            x3t.rearrange("p (t c) -> p t c", c=128), inc="st",
            wait=[("dv", cur("dv"))])
    P5_ST = cur("st")
    tdv = op("vector", lambda E: E.tensor_copy(tmp_db[:, :], tmp_d[:, :]),
             inc="dv", wait=[("dv", cur("dv"))])

    wdw = A_bf[:, 0:NS * 512]
    x3l = A_f2_b[:, 0:NS * 128]
    y3 = [A_f1[:, 0:512], A_f1[:, 512:1024]]
    x1t6 = [A_f1[:, 1024:1536], A_f1[:, 1536:2048]]
    sqd = A_f1[:, 2048:2560]
    outt = A_f1[:, 2560:3072]
    zset = op("vector", lambda E: (E.memset(st_sum[:, :], 0.0),
                                   E.memset(st_sq[:, :], 0.0))[-1],
              inc="dv", wait=[])
    p6_mm = {}
    p6_ev = {}
    for dp in range(8):
        wload = []
        if dp >= 1:
            wload.append(("pe", p6_mm[(dp - 1, NT - 1)]))
        wl = None
        for sblk in range(NS):
            wl = dma(wdw[:, sblk * 512:(sblk + 1) * 512].rearrange("p (m c) -> p m c", c=128),
                     wdT_sc[sblk, dp * 4:dp * 4 + 4].rearrange("m p c -> p m c"),
                     wait=(wload + [("ws", DEQ_WS)]) if sblk == 0 else [])
        dl = dma(dlb_f[:, 0:512], d_lb[:, dp * 512:(dp + 1) * 512])
        dlc = op("vector", lambda E: E.tensor_copy(dlb_t[:, 0:512], dlb_f[:, 0:512]),
                 inc="dv", wait=[("ld", dl), ("dv", tdv)])
        for tt in range(NT):
            xw = [("st", P5_ST)]
            if tt >= 1:
                xw.append(("pe", p6_mm[(dp, tt - 1)]))
            elif dp >= 1:
                xw.append(("pe", p6_mm[(dp - 1, NT - 1)]))
            xl = dma(x3l[:, 0:NS * 128].rearrange("p (s c) -> p s c", c=128),
                     x3_sc[tt].rearrange("s p c -> p s c"), wait=xw)

            def t_dp(E, tt=tt, dp=dp):
                pb = tt % 2
                last = None
                for sblk in range(NS):
                    last = E.matmul(bank(pb), x3l[:, sblk * 128:(sblk + 1) * 128],
                                    wdw[:, sblk * 512:(sblk + 1) * 512],
                                    start=(sblk == 0), stop=False)
                return E.matmul(bank(pb), tmp_db[:, tt * 128:(tt + 1) * 128],
                                dlb_t[0:16, 0:512], start=False, stop=True)
            wmm = [("ld", max(wl, xl)), ("dv", dlc)]
            if (dp, tt - 2) in p6_ev:
                wmm.append(("dv", p6_ev[(dp, tt - 2)]))
            p6_mm[(dp, tt)] = op("tensor", t_dp, inc="pe", wait=wmm)

            x1l = dma(x1t6[tt % 2], x1_sc[tt * 128:(tt + 1) * 128, dp * 512:(dp + 1) * 512],
                      wait=[("dv", p6_ev[(dp, tt - 2)])] if (dp, tt - 2) in p6_ev
                      else [("dv", cur("dv"))])

            def v_y3(E, tt=tt, dp=dp):
                pb = tt % 2
                return E.tensor_add(y3[tt % 2], bank(pb), x1t6[tt % 2])
            w6 = [("pe", p6_mm[(dp, tt)]), ("ld", x1l), ("dv", zset),
                  ("st", cur("st"))]
            yv = op("vector", v_y3, inc="dv", wait=w6)
            p6_ev[(dp, tt)] = yv
            dma(res2_sc[tt * 128:(tt + 1) * 128, dp * 512:(dp + 1) * 512],
                y3[tt % 2], inc="st", wait=[("dv", yv)])
    P6_ST = cur("st")

    ln_a2 = A_f2[:, 0:4096]
    cent2 = A_f1[:, 2048:6144]
    onorm = A_f1[:, 6144:10240]
    for tt in range(NT):
        w = [("st", P6_ST)] if tt == 0 else [("dv", cur("dv"))]
        la = dma(ln_a2, res2_sc[tt * 128:(tt + 1) * 128, :], wait=w)
        sv0 = op("scalar", lambda E: E.activation(out=cent2, in_=ln_a2,
                                                  func=AF.Copy,
                                                  accum_out=stat[:, 0:1]),
                 inc="ev", wait=[("ld", la)])

        dv2s = op("vector", lambda E: E.tensor_scalar(
            out=stat[:, 1:2], in0=stat[:, 0:1], scalar1=float(1.0 / DM),
            scalar2=None, op0=AluOpType.mult), inc="dv", wait=[("ev", sv0)])
        gdv2 = op("gpsimd", lambda E: E.tensor_copy(stat[:, 6:7], stat[:, 1:2]),
                 inc="gg", wait=[("dv", dv2s)])
        dv2 = op("vector", lambda E: E.tensor_sub(
            cent2, ln_a2, stat.ap()[:, 6:7].broadcast_to((128, DM))),
            inc="dv", wait=[("gg", gdv2)])
        sq2 = op("scalar", lambda E: E.activation(out=ln_a2, in_=cent2,
                                                  func=AF.Square,
                                                  accum_out=stat[:, 2:3]),
                 inc="ev", wait=[("dv", dv2)])
        vv2 = op("vector", lambda E: E.tensor_scalar(
            out=stat[:, 3:4], in0=stat[:, 2:3], scalar1=float(1.0 / DM),
            scalar2=float(EPS), op0=AluOpType.mult, op1=AluOpType.add),
            inc="dv", wait=[("ev", sq2)])

        def s_rsq2(E):
            E.activation(out=stat[:, 4:5], in_=stat[:, 3:4], func=AF.Ln)
            return E.activation(out=stat[:, 5:6], in_=stat[:, 4:5],
                                func=AF.Exp, scale=-0.5)
        ssv2 = op("scalar", s_rsq2, inc="ev", wait=[("dv", vv2)])

        def v_o2(E):
            return E.tensor_mul(onorm, cent2,
                                stat.ap()[:, 5:6].broadcast_to((128, DM)))
        wv2 = [("ev", ssv2)]
        if tt >= 1:
            wv2.append(("ot", cur("ot")))
        nv2 = op("vector", v_o2, inc="dv", wait=wv2)
        dma(out[tt * 128:(tt + 1) * 128, :], onorm, inc="ot",
            wait=[("dv", nv2)])
    OT_ALL = cur("ot")
    op("gpsimd", lambda E: E.engine_nop(), inc="gg", wait=[("ot", OT_ALL)])
    if DEBUG_TAPS:
        dbgmr = sbuf("dbgmr", [128, 16], F32)
        d_mr = dram("dbg_mr", [128, 16], F32, "ExternalOutput")
        d_stats = dram("dbg_stats", [128, 16], F32, "ExternalOutput")
        dma(d_mr, dbgmr[:, :], inc="ot", wait=[("gg", cur("gg"))])
        dma(d_stats[:, 0:8], st_sum[:, :], inc="ot")
        dma(d_stats[:, 8:16], st_sq[:, :], inc="ot")
        d_attn = dram("dbg_attn", [OWN, DM], F32, "ExternalOutput")
        d_x1 = dram("dbg_x1", [OWN, DM], F32, "ExternalOutput")
        d_res2 = dram("dbg_res2", [OWN, DM], F32, "ExternalOutput")
        d_wg = dram("dbg_wg", [DMC, 128, 128], BF16, "ExternalOutput")
        d_wd = dram("dbg_wd", [4, 128, 128], BF16, "ExternalOutput")
        d_x3 = dram("dbg_x3", [NS, 128, 128], BF16, "ExternalOutput")
        d_k0 = dram("dbg_k0", [NTS, 128, 128], BF16, "ExternalOutput")
        d_ctx = dram("dbg_ctx", [NT, 128, 128], BF16, "ExternalOutput")
        d_xt = dram("dbg_xt", [DMC, 128, 128], BF16, "ExternalOutput")
        pairs = [(d_attn, attn_sc), (d_x1, x1_sc), (d_res2, res2_sc),
                 (d_wg, wgT_sc[3]), (d_wd, wdT_sc[5, 0:4]), (d_x3, x3_sc[2]),
                 (d_k0, qkvT_sc[32]), (d_ctx, ctxT_dbg if False else qkvT_sc[0, 0:8]),
                 (d_xt, xT_sc[1])]
        w = [("gg", cur("gg"))]
        for dd, ss in pairs:
            dma(dd, ss, inc="ot", wait=w)
            w = None
        op("gpsimd", lambda E: E.engine_nop(), inc="gg", wait=[("ot", cur("ot"))])

    es.enter_context(nc.allow_non_contiguous_dma(
        reason="small strided bias/lora loads; bulk transfers are contiguous"))
    blk = es.enter_context(nc.Block())
    for ename in ENGS:
        thunks = streams[ename]
        getattr(blk, ename)(lambda E, thunks=thunks: [f(E) for f in thunks])
    es.close()
    return nc


_CACHE = {}


def kernel(L0_attn_hidden_states, L0_attn_qkv_w, L0_attn_qkv_b, L0_attn_out_w,
           L0_attn_out_b, L0_ffn_g_codes, L0_ffn_g_absmax, L0_ffn_g_lora_a,
           L0_ffn_g_lora_b, L0_ffn_u_codes, L0_ffn_u_absmax, L0_ffn_u_lora_a,
           L0_ffn_u_lora_b, L0_ffn_d_codes, L0_ffn_d_absmax, L0_ffn_d_lora_a,
           L0_ffn_d_lora_b):
    x = np.asarray(L0_attn_hidden_states, dtype=np.float32)
    B = x.shape[0]
    common = {
        "qkv_w": np.ascontiguousarray(np.asarray(L0_attn_qkv_w, dtype=np.float32)),
        "qkv_b": np.ascontiguousarray(np.asarray(L0_attn_qkv_b, dtype=np.float32)),
        "out_w": np.ascontiguousarray(np.asarray(L0_attn_out_w, dtype=np.float32)),
        "out_b": np.ascontiguousarray(np.asarray(L0_attn_out_b, dtype=np.float32)),
        "g_codes": np.ascontiguousarray(np.asarray(L0_ffn_g_codes, dtype=np.int32)),
        "g_absmax": np.ascontiguousarray(np.asarray(L0_ffn_g_absmax, dtype=np.float32)),
        "u_codes": np.ascontiguousarray(np.asarray(L0_ffn_u_codes, dtype=np.int32)),
        "u_absmax": np.ascontiguousarray(np.asarray(L0_ffn_u_absmax, dtype=np.float32)),
        "d_codes": np.ascontiguousarray(np.asarray(L0_ffn_d_codes, dtype=np.int32)),
        "d_absmax": np.ascontiguousarray(np.asarray(L0_ffn_d_absmax, dtype=np.float32)),
        "g_la": np.ascontiguousarray(np.asarray(L0_ffn_g_lora_a, dtype=np.float32)),
        "g_lb": np.ascontiguousarray(np.asarray(L0_ffn_g_lora_b, dtype=np.float32)),
        "u_la": np.ascontiguousarray(np.asarray(L0_ffn_u_lora_a, dtype=np.float32)),
        "u_lb": np.ascontiguousarray(np.asarray(L0_ffn_u_lora_b, dtype=np.float32)),
        "d_la": np.ascontiguousarray(np.asarray(L0_ffn_d_lora_a, dtype=np.float32)),
        "d_lb": np.ascontiguousarray(np.asarray(L0_ffn_d_lora_b, dtype=np.float32)),
        "ones_row": np.ones((1, 512), dtype=BF16NP),
        "jcol": (np.arange(128, dtype=np.float32) % 64)[:, None],
    }
    in_maps = []
    for core in range(8):
        b, half = core // 2, core % 2
        own = np.arange(half * OWN, (half + 1) * OWN)
        other = np.arange((1 - half) * OWN, (2 - half) * OWN)
        order = np.concatenate([own, other])
        m = dict(common)
        m["hs"] = np.ascontiguousarray(x[b][order])
        m["kpos"] = np.ascontiguousarray(
            np.broadcast_to(order.astype(np.float32)[None, :], (128, S)))
        m["cmask"] = np.full((1, 128), MASKV if half == 0 else 0.0, dtype=BF16NP)
        in_maps.append(m)

    nc = build_program()
    res = run_bass_kernel_spmd(nc, in_maps, core_ids=list(range(8)))
    outp = np.empty((B, S, DM), dtype=np.float32)
    for core in range(8):
        b, half = core // 2, core % 2
        outp[b, half * OWN:(half + 1) * OWN, :] = res.results[core]["out"]
    return outp
